# version 23
# baseline (speedup 1.0000x reference)
import numpy as np
import ml_dtypes
from contextlib import ExitStack
import concourse.bass as bass
import concourse.mybir as mybir
from concourse.bass_utils import run_bass_kernel_spmd

F32 = mybir.dt.float32
BF16 = mybir.dt.bfloat16
I32 = mybir.dt.int32
AF = mybir.ActivationFunctionType
ALU = mybir.AluOpType
AX = mybir.AxisListType
BF = ml_dtypes.bfloat16

ENGINES = ("tensor", "vector", "scalar", "gpsimd", "sync")
EPS = 1e-5


class _Op:
    __slots__ = ("engine", "fn", "sem", "inc_amt", "need_inc", "waits", "idx", "tag")

    def __init__(self, engine, fn, sem, inc_amt, idx, tag):
        self.engine = engine
        self.fn = fn
        self.sem = sem
        self.inc_amt = inc_amt
        self.need_inc = False
        self.waits = []
        self.idx = idx
        self.tag = tag


class Emitter:

    def __init__(self, nc, es, n_dma_slots=28):
        self.nc = nc
        self.ops = []
        self.eng_sems = {e: es.enter_context(nc.semaphore(f"sem_{e}")) for e in ENGINES}
        self.dma_sems = [es.enter_context(nc.semaphore(f"dma{i}")) for i in range(n_dma_slots)]
        self.dma_slot_last = [None] * n_dma_slots
        self.last_writer = {}
        self.readers = {}
        self.eng_wm = {e: {} for e in ENGINES}
        self.fence_deps = []

    def _mkop(self, engine, fn, sem, inc_amt, tag):
        op = _Op(engine, fn, sem, inc_amt, len(self.ops), tag)
        self.ops.append(op)
        return op

    def _add_dep(self, op, dep):
        if dep is None or dep is op:
            return
        if dep.engine == op.engine == "tensor" and dep.sem is op.sem:
            return
        key = id(dep.sem)
        if self.eng_wm[op.engine].get(key, -1) >= dep.idx:
            return
        dep.need_inc = True
        op.waits.append(dep)
        self.eng_wm[op.engine][key] = dep.idx

    def _track(self, op, reads, writes):
        for d in self.fence_deps:
            self._add_dep(op, d)
        for r in reads:
            self._add_dep(op, self.last_writer.get(r))
        for w in writes:
            self._add_dep(op, self.last_writer.get(w))
            for rd in self.readers.get(w, {}).values():
                self._add_dep(op, rd)
        for r in reads:
            self.readers.setdefault(r, {})[(op.engine, id(op.sem))] = op
        for w in writes:
            self.last_writer[w] = op
            self.readers[w] = {}

    def emit(self, engine, fn, reads=(), writes=(), tag=""):
        op = self._mkop(engine, fn, self.eng_sems[engine], 1, tag)
        self._track(op, reads, writes)
        return op

    def dma(self, qengine, slot, out, in_, reads=(), writes=(), tag="", slow_ok=False):
        def fn(eng, _o=out, _i=in_, _s=slow_ok):
            return eng.dma_start(out=_o, in_=_i, allow_slow_non_contiguous=_s)

        op = self._mkop(qengine, fn, self.dma_sems[slot], 16, tag)
        op.need_inc = True
        prev = self.dma_slot_last[slot]
        if prev is not None:
            self._add_dep(op, prev)
        self._track(op, reads, writes)
        self.dma_slot_last[slot] = op
        return op

    def fence(self):
        fl = []
        for e in ENGINES:
            for o in reversed(self.ops):
                if o.engine == e:
                    fl.append(o)
                    break
        fl += [p for p in self.dma_slot_last if p is not None]
        self.fence_deps = fl
        self.last_writer.clear()
        self.readers.clear()

    def barrier(self, engine="sync"):
        op = self._mkop(engine, lambda eng: None, self.eng_sems[engine], 0, "barrier")
        for prev in self.dma_slot_last:
            if prev is not None:
                self._add_dep(op, prev)
        for e in ENGINES:
            if e == engine:
                continue
            for o in reversed(self.ops):
                if o.engine == e:
                    self._add_dep(op, o)
                    break
        return op

    def finalize(self, block):
        sem_count = {}
        op_count = {}
        for op in self.ops:
            if op.need_inc:
                k = id(op.sem)
                sem_count[k] = sem_count.get(k, 0) + op.inc_amt
                op_count[op] = sem_count[k]
        per_engine = {e: [] for e in ENGINES}
        for op in self.ops:
            per_engine[op.engine].append(op)

        def mk(ops):
            def run(eng):
                for op in ops:
                    for dep in op.waits:
                        eng.wait_ge(dep.sem, op_count[dep])
                    inst = op.fn(eng)
                    if op.need_inc:
                        if inst is None:
                            raise RuntimeError(f"op {op.tag} needs inc, no instruction")
                        inst.then_inc(op.sem, op.inc_amt)
            return run

        for e in ENGINES:
            if per_engine[e]:
                getattr(block, e)(mk(per_engine[e]))
        return {"ops": len(self.ops),
                "waits": sum(len(o.waits) for o in self.ops),
                "incs": sum(1 for o in self.ops if o.need_inc)}


class Arena:

    def __init__(self, nc, es, name, dt, n_elems):
        self.t = es.enter_context(nc.sbuf_tensor(name, [128, n_elems], dt))
        self.n = n_elems
        self.cur = 0

    def set_base(self, base):
        self.cur = base

    def alloc(self, elems, shape=None, rows=128, off=None):
        if off is None:
            off = self.cur
            self.cur += elems
        assert off + elems <= self.n, f"arena {self.t.name} overflow {off}+{elems}>{self.n}"
        ap = self.t[0:rows, off:off + elems]
        if shape is not None:
            pat = "p (" + " ".join(f"d{i}" for i in range(len(shape))) + ") -> p " + \
                  " ".join(f"d{i}" for i in range(len(shape)))
            kw = {f"d{i}": s for i, s in enumerate(shape)}
            ap = ap.rearrange(pat, **kw)
        return ap


class Cfg:
    def __init__(self, D=2048, S_own=1024, S_all=2048, I=8192, FWIN=128, DBG=()):
        self.D, self.S_own, self.S_all, self.I, self.FWIN = D, S_own, S_all, I, FWIN
        self.DC = D // 128
        self.H = D // 128
        self.TT = S_own // 128
        self.TA = S_all // 128
        self.AC = I // 128
        self.NW = S_own // FWIN
        self.DBG = set(DBG)
        assert self.AC & (self.AC - 1) == 0 and self.NW >= 1


def build_program(cfg: Cfg):
    D, DC, H = cfg.D, cfg.DC, cfg.H
    S_own, S_all, TT, TA = cfg.S_own, cfg.S_all, cfg.TT, cfg.TA
    I, AC = cfg.I, cfg.AC
    FWIN, NW = cfg.FWIN, cfg.NW
    ST = min(512, S_own)
    NS = S_own // ST
    C_FW = float(np.float32(1.0) / np.float32(np.sqrt(np.float32(I))))
    NBN = max(D // 512, 1)
    WBN = min(D, 512)

    nc = bass.Bass("TRN2", target_bir_lowering=False)

    def din(name, shape, dt=F32):
        return nc.dram_tensor(name, list(shape), dt, kind="ExternalInput").ap()

    x_own = din("x_own", [S_own, D])
    x_oth = din("x_oth", [S_all - S_own, D])
    g_normB = din("g_normB", [128, D])
    g_outB = din("g_outB", [128, D])
    Wq = din("Wq", [D, D])
    Wkv = din("Wkv", [D, 256])
    Wout = din("Wout", [D, D])
    Wg = din("Wg", [I, D])
    Wu = din("Wu", [I, D])
    Wd = din("Wd", [D, I])
    IDN = din("IDN", [128, 128], BF16)
    IDN32 = din("IDN32", [128, 128], F32)
    H128 = din("H128", [128, 128], BF16)

    OUT = nc.dram_tensor("OUT", [S_own, D], F32, kind="ExternalOutput").ap()
    dbgs = {}

    def dbg(name, shape, dt=F32):
        if name in cfg.DBG:
            dbgs[name] = nc.dram_tensor("dbg_" + name, list(shape), dt,
                                        kind="ExternalOutput").ap()
            return dbgs[name]
        return None

    xn_scr = nc.dram_tensor("xn_scr", [S_own, D], F32)
    y1_scr = nc.dram_tensor("y1_scr", [128, AC, S_own], F32)
    y2_scr = nc.dram_tensor("y2_scr", [128, AC, S_own], BF16)
    g_scr = nc.dram_tensor("g_scr", [S_own, D], F32)
    ao_scr = nc.dram_tensor("ao_scr", [S_own, D], F32)
    rowscr = nc.dram_tensor("rowscr", [2, TT, 128], F32)

    es = ExitStack()
    with es:
        em = Emitter(nc, es, n_dma_slots=28)
        sbt = lambda nm, shp, dt: es.enter_context(nc.sbuf_tensor(nm, shp, dt))
        pst = lambda nm, shp, dt: es.enter_context(nc.psum_tensor(nm, shp, dt))
        E, DMA = em.emit, em.dma
        V, S, T, G = "vector", "scalar", "tensor", "gpsimd"

        sid = sbt("sid", [128, 128], BF16)
        sid32 = sbt("sid32", [128, 128], F32)
        sh128 = sbt("sh128", [128, 128], BF16)
        khT = sbt("khT", [128, S_all], BF16)
        vT1 = sbt("vT1", [128, TA, 132], BF16)
        stats6 = sbt("stats6", [128, 2, NBN, 6], F32)
        stats2 = sbt("stats2", [128, 2, 2], F32)
        rstd = sbt("rstd", [128, 2], F32)
        sqv = sbt("sqv", [128, 2], F32)
        mbv = sbt("mbv", [128, 2], F32)
        epsB = sbt("epsB", [128, 1], F32)
        KofsB = sbt("KofsB", [128, 1], F32)
        ktrial = sbt("ktrial", [128, 1], F32)
        ktriali = sbt("ktriali", [128, 1], I32)
        negh = sbt("negh", [128, 1], F32)
        posh = sbt("posh", [128, 1], F32)
        onesc = sbt("onesc", [128, 1], BF16)
        ones_r = sbt("ones_r", [1, 128], BF16)
        amaxv = sbt("amaxv", [128, 1], F32)
        sclv = sbt("sclv", [128, 1], F32)
        recips = sbt("recips", [128, 1], F32)
        tmax = sbt("tmax", [128, 2], F32)
        zs_tm = sbt("zs_tm", [128, 2], F32)
        wabs = sbt("wabs", [128, 1], F32)
        wsum = sbt("wsum", [128, 4], F32)
        wsum16 = sbt("wsum16", [128, 1], BF16)
        wsumlo = sbt("wsumlo", [128, 1], BF16)
        mrow = sbt("mrow", [1, 4], F32)
        mrow16 = sbt("mrow16", [1, 4], BF16)
        mrowlo = sbt("mrowlo", [1, 4], BF16)
        mB = sbt("mB", [128, 4], F32)
        rmB = sbt("rmB", [128, 4], F32)
        ainv_all = sbt("ainv_all", [128, max(TT, 2)], F32)
        s2i_all = sbt("s2i_all", [128, max(TT, 2)], F32)

        AFN = max(
            6 * D + D + 3 * S_all + 512,
            4 * S_own + 2 * ST + 2 * S_own + 4 * D + D + 2 * DC * 128,
            2 * S_own + 4 * D + 4 * D + 5 * S_own,
            2 * AC * FWIN + 8 * FWIN + 6 * FWIN,
            I + I + 3 * S_own + 256 + 2 * S_own,
            11 * D,
            4 * S_own + 2 * ST + 2 * S_own)
        ABN = H * S_own + max(
            TA * S_own + TT * H * 132,
            DC * S_own + 2 * D + 4 * DC * 128 + 2 * S_own + 2 * S_own,
            I + 6 * S_own + 2 * AC * 128 + 1024,
            DC * S_own + 2 * D + DC * 128 + DC * 256 + S_all + 2048,
            DC * S_own + 2 * DC * 128 + 4 * S_own + 2 * ST)
        AIN = max(D // 2, 8 * FWIN)
        AFa = Arena(nc, es, "AF", F32, AFN)
        ABa = Arena(nc, es, "AB", BF16, ABN)
        AIa = Arena(nc, es, "AI", I32, AIN)
        qhT = ABa.alloc(H * S_own, (H, S_own), off=0)
        AB_P = H * S_own

        psA = pst("psA", [128, 1024], F32)
        psB = pst("psB", [128, 1024], F32)
        psC = pst("psC", [128, 512], F32)
        psD = pst("psD", [128, 512], F32)
        psT = pst("psT", [128, 1024], BF16)
        psU = pst("psU", [128, 1024], BF16)
        A_, B_, C_, D_, T_, U_ = "psA", "psB", "psC", "psD", "psT", "psU"

        DMA("sync", 26, sid[:], IDN, writes=["sid"])
        DMA("sync", 26, sid32[:], IDN32, writes=["sid32"])
        DMA("sync", 26, sh128[:], H128, writes=["sh128"])
        E(V, lambda v: v.memset(epsB[:], EPS), writes=["epsB"])
        E(V, lambda v: v.memset(onesc[:], 1.0), writes=["onesc"])
        E(V, lambda v: v.memset(ones_r[:], 1.0), writes=["ones_r"])
        E(V, lambda v: v.memset(negh[:], -0.5), writes=["negh"])
        E(V, lambda v: v.memset(posh[:], 0.5), writes=["posh"])
        E(V, lambda v: v.memset(vT1[:], 0.0), writes=[("vT1", -1)])
        E(V, lambda v: v.memset(vT1[:, :, 128:129], 1.0), reads=[("vT1", -1)], writes=["vT1ones"])
        E(V, lambda v: v.memset(ktrial[:], 0.6), writes=["ktrial"])
        E(V, lambda v: v.tensor_copy(ktriali[:], ktrial[:]), reads=["ktrial"], writes=["ktriali"])
        E(V, lambda v: v.tensor_scalar(KofsB[:], ktriali[:], -0.5, 0.5, ALU.mult, ALU.add),
          reads=["ktriali"], writes=["KofsB"])

        def ln_stats(src_key, src_ap, tag, par=0):
            for c in range(NBN):
                E(V, lambda v, c=c: v.bn_stats(stats6[:, par, c, :], src_ap[:, c * WBN:(c + 1) * WBN]),
                  reads=[src_key], writes=[("st6", par, c)], tag=f"bnst{tag}")
            E(V, lambda v: v.bn_aggr(stats2[:, par, :], stats6[:, par, 0:NBN, :]),
              reads=[("st6", par, c) for c in range(NBN)], writes=[("stats2", par)], tag=f"bnag{tag}")
            E(S, lambda s: s.activation(sqv[:, par:par + 1], stats2[:, par, 1:2], AF.Sqrt,
                                        bias=epsB[:, 0:1]),
              reads=[("stats2", par), "epsB"], writes=[("sqv", par)], tag=f"sq{tag}")
            E(V, lambda v: v.reciprocal(rstd[:, par:par + 1], sqv[:, par:par + 1]),
              reads=[("sqv", par)], writes=[("rstd", par)])
            E(V, lambda v: v.scalar_tensor_tensor(mbv[:, par:par + 1], stats2[:, par, 0:1], -1.0,
                                                  rstd[:, par:par + 1], ALU.mult, ALU.mult),
              reads=[("stats2", par), ("rstd", par)], writes=[("mbv", par)], tag=f"mb{tag}")

        def transpose16(src_key, src_ap, dst_key, dst_ap3, nchunks, tag, ps, pk):
            done = 0
            while done < nchunks:
                nb = min(8, nchunks - done)
                for j in range(nb):
                    c = done + j
                    E(T, lambda t, c=c, j=j: t.transpose(ps[:, j * 128:(j + 1) * 128],
                                                         src_ap[:, c * 128:(c + 1) * 128], sid[:]),
                      reads=[src_key, "sid"], writes=[pk], tag=f"tr{tag}")
                E(V, lambda v, done=done, nb=nb: v.tensor_copy(dst_ap3[:, done:done + nb, :],
                                                               ps[:, 0:nb * 128]),
                  reads=[pk], writes=[dst_key], tag=f"trev{tag}")
                done += nb

        def bcast_row(row_key, row_ap, r16, r16k, rlo, rlok, dst_key, dst_ap, Wn, psel, tag):
            E(V, lambda v: v.tensor_copy(r16[0:1, 0:Wn], row_ap), reads=[row_key], writes=[r16k],
              tag=f"b16{tag}")
            E(V, lambda v: v.scalar_tensor_tensor(rlo[0:1, 0:Wn], row_ap, 1.0, r16[0:1, 0:Wn],
                                                  ALU.mult, ALU.subtract),
              reads=[row_key, r16k], writes=[rlok], tag=f"blo{tag}")
            ps, pk = psel
            for j in range((Wn + 511) // 512):
                w0, w1 = j * 512, min(Wn, (j + 1) * 512)
                E(T, lambda t, w0=w0, w1=w1: t.matmul(ps[:, 0:w1 - w0], ones_r[:], r16[0:1, w0:w1],
                                                      start=True, stop=False),
                  reads=["ones_r", r16k], writes=[pk], tag=f"bmm{tag}")
                E(T, lambda t, w0=w0, w1=w1: t.matmul(ps[:, 0:w1 - w0], ones_r[:], rlo[0:1, w0:w1],
                                                      start=False, stop=True),
                  reads=["ones_r", rlok], writes=[pk], tag=f"bmm2{tag}")
                E(V, lambda v, w0=w0, w1=w1: v.tensor_copy(dst_ap[:, w0:w1], ps[:, 0:w1 - w0]),
                  reads=[pk], writes=[dst_key], tag=f"bev{tag}")

        def ternarize(src_key, src_ap, dst_key, dst_ap, Wn, mcol, g1, g1k, g2, g2k, tag):
            E(S, lambda s: s.activation(g1[:, 0:Wn], src_ap, AF.Sign, bias=negh[:, 0:1],
                                        scale=rmB[:, mcol:mcol + 1]),
              reads=[src_key, "negh", "rmB"], writes=[g1k], tag=f"tg1{tag}")
            E(S, lambda s: s.activation(g2[:, 0:Wn], src_ap, AF.Sign, bias=posh[:, 0:1],
                                        scale=rmB[:, mcol:mcol + 1]),
              reads=[src_key, "posh", "rmB"], writes=[g2k], tag=f"tg2{tag}")
            E(V, lambda v: v.tensor_add(g1[:, 0:Wn], g1[:, 0:Wn], g2[:, 0:Wn]),
              reads=[g1k, g2k], writes=[g1k], tag=f"tadd{tag}")
            E(V, lambda v: v.tensor_scalar_mul(dst_ap, g1[:, 0:Wn], 0.5),
              reads=[g1k], writes=[dst_key], tag=f"thalf{tag}")

        ABa.set_base(AB_P)
        xaT = ABa.alloc(DC * S_own, (DC, S_own))
        xab = ABa.alloc(2 * D, (2, D))
        xoT = ABa.alloc(DC * 128, (DC, 128))
        wkvT = ABa.alloc(DC * 256, (DC, 256))
        vball = ABa.alloc(S_all)
        ksq16a = ABa.alloc(512)
        ksqloa = ABa.alloc(512)
        krr16 = ABa.alloc(512, rows=1)
        krrlo = ABa.alloc(512, rows=1)
        AFa.set_base(0)
        xtile = AFa.alloc(2 * D, (2, D))
        xn = AFa.alloc(2 * D, (2, D))
        xa = AFa.alloc(2 * D, (2, D))
        gnB = AFa.alloc(D)
        kall = AFa.alloc(S_all)
        ksqall = AFa.alloc(S_all)
        rbckall = AFa.alloc(S_all)
        rrk = AFa.alloc(512, rows=1)

        DMA("sync", 26, gnB, g_normB, writes=["gnB"])
        DMA("gpsimd", 24, wkvT, Wkv.rearrange("(c p) n -> p c n", p=128), writes=["wkvT"])

        d_xn = dbg("xn", [TT, 128, D])
        d_xa = dbg("xa", [TA, 128, D])
        d_kh = dbg("kh", [128, S_all], BF16)
        d_vt = dbg("vt", [128, TA, 132], BF16)
        d_qh = dbg("qh", [128, H, S_own], BF16)

        def ldx(i):
            own_ = i < TT
            s_ = (x_own if own_ else x_oth).rearrange("(t p) d -> t p d", p=128)[i if own_ else i - TT]
            DMA("sync", i % 2, xtile[:, i % 2, :], s_, writes=[("xtile", i % 2)], tag=f"ldx{i}")
        ldx(0)
        if TA > 1:
            ldx(1)
        for i in range(TA):
            own = i < TT
            par = i % 2
            xk = ("xtile", par)
            xt_ap = xtile[:, par, :]
            ln_stats(xk, xt_ap, f"l1_{i}", par)
            E(V, lambda v, xt_ap=xt_ap, par=par: v.tensor_scalar(xn[:, par, :], xt_ap, rstd[:, par:par + 1],
                                                                 mbv[:, par:par + 1], ALU.mult, ALU.add),
              reads=[xk, ("rstd", par), ("mbv", par)], writes=[("xn", par)], tag="ln1ap")
            if own:
                DMA("sync", 10 + par, xn_scr.ap().rearrange("(t p) d -> t p d", p=128)[i], xn[:, par, :],
                    reads=[("xn", par)], writes=[("xn_scr",)], tag=f"stxn{i}")
                if d_xn is not None:
                    DMA("sync", 25, d_xn[i], xn[:, par, :], reads=[("xn", par)])
            ln_stats(("xn", par), xn[:, par, :], f"l2_{i}", par)
            E(V, lambda v, par=par: v.tensor_scalar(xa[:, par, :], xn[:, par, :], rstd[:, par:par + 1],
                                                    mbv[:, par:par + 1], ALU.mult, ALU.add),
              reads=[("xn", par), ("rstd", par), ("mbv", par)], writes=[("xa", par)], tag="ln2ap")
            E(V, lambda v, par=par: v.tensor_mul(xa[:, par, :], xa[:, par, :], gnB),
              reads=[("xa", par), "gnB"], writes=[("xa", par)])
            E(V, lambda v, par=par: v.tensor_copy(xab[:, par, :], xa[:, par, :]),
              reads=[("xa", par)], writes=[("xab", par)])
            if d_xa is not None:
                DMA("sync", 25, d_xa[i], xa[:, par, :], reads=[("xa", par)])
            if own:
                transpose16(("xab", par), xab[:, par, :], "xaT",
                            xaT[:, :, i * 128:(i + 1) * 128], DC, f"xa{i}", psU, U_)
                rk, rget = "xaT", (lambda c, i=i: xaT[:, c, i * 128:(i + 1) * 128])
            else:
                transpose16(("xab", par), xab[:, par, :], "xoT", xoT[:, :, :], DC, f"xo{i}", psU, U_)
                rk, rget = "xoT", (lambda c: xoT[:, c, :])
            for half in range(2):
                for c in range(DC):
                    E(T, lambda t, c=c, half=half, rget=rget: t.matmul(
                        psC[:, half * 128:(half + 1) * 128], wkvT[:, c, half * 128:(half + 1) * 128],
                        rget(c), start=(c == 0), stop=(c == DC - 1)),
                      reads=["wkvT", rk], writes=[C_], tag="kvmm")
            E(V, lambda v, i=i: v.tensor_copy(kall[:, i * 128:(i + 1) * 128], psC[:, 0:128]),
              reads=[C_], writes=["kall"], tag="kev")
            E(V, lambda v, i=i: v.tensor_copy(vball[:, i * 128:(i + 1) * 128], psC[:, 128:256]),
              reads=[C_], writes=["vball"], tag="vev0")
            if i + 2 < TA:
                ldx(i + 2)
        E(S, lambda s: s.activation(ksqall, kall, AF.Square), reads=["kall"], writes=["ksqall"])
        for j4 in range(S_all // 512):
            E(V, lambda v, j4=j4: v.tensor_copy(ksq16a, ksqall[:, j4 * 512:(j4 + 1) * 512]),
              reads=["ksqall"], writes=["ksq16a"])
            E(V, lambda v, j4=j4: v.scalar_tensor_tensor(ksqloa, ksqall[:, j4 * 512:(j4 + 1) * 512], 1.0,
                                                         ksq16a, ALU.mult, ALU.subtract),
              reads=["ksqall", "ksq16a"], writes=["ksqloa"])
            E(T, lambda t: t.matmul(psD[0:1, 0:512], onesc[:], ksq16a, start=True, stop=False),
              reads=["onesc", "ksq16a"], writes=[D_], tag="ksmm")
            E(T, lambda t: t.matmul(psD[0:1, 0:512], onesc[:], ksqloa, start=False, stop=True),
              reads=["onesc", "ksqloa"], writes=[D_], tag="ksmm2")
            E(V, lambda v, j4=j4: v.tensor_scalar(rrk, psD[0:1, 0:512], 1.0, 1e-12, ALU.mult, ALU.add),
              reads=[D_], writes=["rrk"], tag="kssft")
            E(S, lambda s: s.activation(rrk, rrk, AF.Sqrt), reads=["rrk"], writes=["rrk"])
            E(V, lambda v: v.reciprocal(rrk, rrk), reads=["rrk"], writes=["rrk"])
            E(V, lambda v: v.tensor_scalar_mul(rrk, rrk, 4.0), reads=["rrk"], writes=["rrk"])
            bcast_row("rrk", rrk, krr16, "krr16", krrlo, "krrlo",
                      "rbckall", rbckall[:, j4 * 512:(j4 + 1) * 512], 512, (psD, D_), f"kb{j4}")
        E(V, lambda v: v.tensor_mul(khT[:], kall, rbckall), reads=["kall", "rbckall"], writes=["khT"],
          tag="khmul")
        for i0 in range(0, TA, 8):
            nb = min(8, TA - i0)
            for j in range(nb):
                E(T, lambda t, i0=i0, j=j: t.transpose(psT[:, j * 128:(j + 1) * 128],
                                                       vball[:, (i0 + j) * 128:(i0 + j + 1) * 128], sid[:]),
                  reads=["vball", "sid"], writes=[T_], tag="vtr")
            E(V, lambda v, i0=i0, nb=nb: v.tensor_copy(vT1[:, i0:i0 + nb, 0:128], psT[:, 0:nb * 128]),
              reads=[T_, ("vT1", -1)], writes=[("vT1", i) for i in range(i0, i0 + nb)], tag="vev")
        if d_kh is not None:
            DMA("sync", 25, d_kh, khT[:], reads=["khT"])
        if d_vt is not None:
            DMA("sync", 25, d_vt, vT1[:], reads=[("vT1", i) for i in range(TA)] + ["vT1ones"])

        em.fence()
        ABa.set_base(AB_P)
        xaT2 = ABa.alloc(DC * S_own, (DC, S_own))
        wq_slab = ABa.alloc(2 * DC * 128, (2, DC, 128))
        qsq16 = ABa.alloc(2 * S_own, (2, S_own))
        qsqlo = ABa.alloc(2 * S_own, (2, S_own))
        rr16q = ABa.alloc(ST, rows=1)
        rrloq = ABa.alloc(ST, rows=1)
        AFa.set_base(0)
        qraw = AFa.alloc(2 * S_own, (2, S_own))
        qsq = AFa.alloc(2 * S_own, (2, S_own))
        rbc = AFa.alloc(2 * ST, (2, ST))
        rrow = AFa.alloc(2 * S_own, (2, S_own), rows=1)

        wf_slab = AFa.alloc(4 * D, (4, D))
        dumpb = AFa.alloc(D)
        wq_stg0 = AFa.alloc(2 * DC * 128, (2, DC, 128))
        mean_items = []
        for col, Wap, n_el in ((0, Wg, float(I * D)), (1, Wu, float(I * D)), (2, Wd, float(D * I))):
            if Wap is Wd:
                for si in range(D // 128):
                    for q in range(I // D):
                        mean_items.append((col, Wap.rearrange("(s p) i -> s p i", p=128)[si]
                                           [:, q * D:(q + 1) * D]))
            else:
                for si in range(I // 128):
                    mean_items.append((col, Wap.rearrange("(s p) i -> s p i", p=128)[si]))
        _mi_counter = [0]

        def emit_mean(n):
            for _ in range(n):
                k = _mi_counter[0]
                if k >= len(mean_items):
                    return
                _mi_counter[0] += 1
                col, ap_ = mean_items[k]
                sl = k % 4
                DMA("sync", 4 + sl, wf_slab[:, sl, :], ap_, writes=[("wfb", sl)], tag=f"ldm{k}")
                E(S, lambda s, sl=sl: s.activation(dumpb, wf_slab[:, sl, :], AF.Abs,
                                                   accum_out=wabs[:, 0:1]),
                  reads=[("wfb", sl), "dumpb"], writes=["dumpb", "wabs"], tag="mabs")
                E(V, lambda v, col=col: v.tensor_add(wsum[:, col:col + 1], wsum[:, col:col + 1],
                                                     wabs[:, 0:1]),
                  reads=["wsum", "wabs"], writes=["wsum"], tag="macc")

        E(V, lambda v: v.memset(wsum[:], 0.0), writes=["wsum"])

        def ldwq(h):
            DMA("sync", 8 + h % 2, wq_stg0[:, h % 2, :, :],
                Wq[:, h * 128:(h + 1) * 128].rearrange("(c p) m -> p c m", p=128),
                writes=[("wqs", h % 2)], tag=f"ldwq{h}")
            E(V, lambda v, h=h: v.tensor_copy(wq_slab[:, h % 2, :, :], wq_stg0[:, h % 2, :, :]),
              reads=[("wqs", h % 2)], writes=[("wq", h % 2)], tag=f"cwq{h}")
        ldwq(0)
        if H > 1:
            ldwq(1)
        for h in range(H):
            hp = h % 2
            wk = ("wq", hp)
            for c in range(DC):
                for ns in range(NS):
                    E(T, lambda t, c=c, ns=ns, hp=hp: t.matmul(
                        psA[:, ns * ST:(ns + 1) * ST], wq_slab[:, hp, c, :],
                        xaT2[:, c, ns * ST:(ns + 1) * ST], start=(c == 0), stop=(c == DC - 1)),
                      reads=[wk, "xaT"], writes=[A_], tag="qmm")
            E(V, lambda v, hp=hp: v.tensor_copy(qraw[:, hp, :], psA[:, 0:S_own]),
              reads=[A_], writes=[("qraw", hp)])
            E(S, lambda s, hp=hp: s.activation(qsq[:, hp, :], qraw[:, hp, :], AF.Square),
              reads=[("qraw", hp)], writes=[("qsq", hp)])
            E(V, lambda v, hp=hp: v.tensor_copy(qsq16[:, hp, :], qsq[:, hp, :]),
              reads=[("qsq", hp)], writes=[("qsq16", hp)])
            E(V, lambda v, hp=hp: v.scalar_tensor_tensor(qsqlo[:, hp, :], qsq[:, hp, :], 1.0,
                                                         qsq16[:, hp, :], ALU.mult, ALU.subtract),
              reads=[("qsq", hp), ("qsq16", hp)], writes=[("qsqlo", hp)])
            for ns in range(NS):
                E(T, lambda t, ns=ns, hp=hp: t.matmul(psD[0:1, 0:ST], onesc[:],
                                                      qsq16[:, hp, ns * ST:(ns + 1) * ST],
                                                      start=True, stop=False),
                  reads=["onesc", ("qsq16", hp)], writes=[D_], tag="qsmm")
                E(T, lambda t, ns=ns, hp=hp: t.matmul(psD[0:1, 0:ST], onesc[:],
                                                      qsqlo[:, hp, ns * ST:(ns + 1) * ST],
                                                      start=False, stop=True),
                  reads=["onesc", ("qsqlo", hp)], writes=[D_], tag="qsmm2")
                E(V, lambda v, ns=ns, hp=hp: v.tensor_scalar(rrow[0:1, hp, ns * ST:(ns + 1) * ST],
                                                             psD[0:1, 0:ST], 256.0, 1e-12,
                                                             ALU.mult, ALU.add),
                  reads=[D_], writes=[("rrow", hp)], tag="qsc")
            E(S, lambda s, hp=hp: s.activation(rrow[0:1, hp, :], rrow[0:1, hp, :], AF.Sqrt),
              reads=[("rrow", hp)], writes=[("rrow", hp)])
            E(V, lambda v, hp=hp: v.reciprocal(rrow[0:1, hp, :], rrow[0:1, hp, :]),
              reads=[("rrow", hp)], writes=[("rrow", hp)])
            E(V, lambda v, hp=hp: v.tensor_scalar_mul(rrow[0:1, hp, :], rrow[0:1, hp, :], 64.0),
              reads=[("rrow", hp)], writes=[("rrow", hp)])
            for ns in range(NS):
                bcast_row(("rrow", hp), rrow[0:1, hp, ns * ST:(ns + 1) * ST], rr16q, "rr16", rrloq, "rrlo",
                          ("rbc", hp), rbc[:, hp, :], ST, (psD, D_), f"qb{h}_{ns}")
                E(V, lambda v, h=h, ns=ns, hp=hp: v.tensor_mul(qhT[:, h, ns * ST:(ns + 1) * ST],
                                                               qraw[:, hp, ns * ST:(ns + 1) * ST],
                                                               rbc[:, hp, 0:ST]),
                  reads=[("qraw", hp), ("rbc", hp)], writes=["qhT"], tag="qhmul")
            if h + 2 < H:
                ldwq(h + 2)
            emit_mean(len(mean_items) // H + 1)
        if d_qh is not None:
            DMA("sync", 25, d_qh, qhT[:, :, :], reads=["qhT"])

        emit_mean(len(mean_items))

        def mean_finish(col, n_el, tag):
            E(V, lambda v, col=col: v.tensor_copy(wsum16[:], wsum[:, col:col + 1]),
              reads=["wsum"], writes=["wsum16"])
            E(V, lambda v, col=col: v.scalar_tensor_tensor(wsumlo[:], wsum[:, col:col + 1], 1.0,
                                                           wsum16[:], ALU.mult, ALU.subtract),
              reads=["wsum", "wsum16"], writes=["wsumlo"])
            E(T, lambda t: t.matmul(psD[0:1, 0:1], wsum16[:], onesc[:], start=True, stop=False),
              reads=["wsum16", "onesc"], writes=[D_], tag=f"ms{tag}")
            E(T, lambda t: t.matmul(psD[0:1, 0:1], wsumlo[:], onesc[:], start=False, stop=True),
              reads=["wsumlo", "onesc"], writes=[D_], tag=f"ms2{tag}")
            E(V, lambda v, col=col, n_el=n_el: v.tensor_scalar(mrow[0:1, col:col + 1], psD[0:1, 0:1],
                                                               1.0 / n_el, EPS, ALU.mult, ALU.max),
              reads=[D_], writes=["mrow"], tag=f"mf{tag}")

        mean_finish(0, float(I * D), "g")
        mean_finish(1, float(I * D), "u")
        mean_finish(2, float(D * I), "d")
        bcast_row("mrow", mrow[0:1, 0:3], mrow16, "m16", mrowlo, "mlo", "mB", mB, 3, (psD, D_), "mb")
        E(V, lambda v: v.reciprocal(rmB[:, 0:3], mB[:, 0:3]), reads=["mB"], writes=["rmB"])

        em.fence()
        ABa.set_base(AB_P)
        iaT = ABa.alloc(DC * S_own, (DC, S_own))
        AB_P1B = ABa.cur
        qbf = ABa.alloc(D)
        ABa.set_base(AB_P1B)
        wt_bf = ABa.alloc(2 * D, (2, D))
        wtT = ABa.alloc(2 * 2 * DC * 128, (2, 2, DC, 128))
        hi_s = ABa.alloc(S_own)
        lo_s = ABa.alloc(S_own)
        arow16 = ABa.alloc(S_own, rows=1)
        arowlo = ABa.alloc(S_own, rows=1)
        AFa.set_base(0)
        aivB = AFa.alloc(S_own)
        arow = AFa.alloc(S_own, rows=1)
        xnq = AFa.alloc(D)
        sgn = AFa.alloc(D)
        qint = AIa.alloc(D // 2, off=0)

        d_iq = dbg("iq", [TT, 128, D], BF16)
        for i in range(TT):
            xk = ("xnq",)
            DMA("sync", i % 2, xnq, xn_scr.ap().rearrange("(t p) d -> t p d", p=128)[i],
                reads=[("xn_scr",)], writes=[xk], tag=f"ldxn{i}")
            E(V, lambda v: v.tensor_reduce(amaxv[:], xnq, AX.X, ALU.max, apply_absolute_value=True),
              reads=[xk], writes=["amaxv"], tag="amax")
            E(V, lambda v: v.tensor_scalar_max(amaxv[:], amaxv[:], EPS), reads=["amaxv"], writes=["amaxv"])
            E(V, lambda v, i=i: v.tensor_scalar_mul(ainv_all[:, i:i + 1], amaxv[:], 1.0 / 128.0),
              reads=["amaxv"], writes=["ainv_all"])
            E(V, lambda v: v.reciprocal(sclv[:], amaxv[:]), reads=["amaxv"], writes=["sclv"])
            E(V, lambda v: v.tensor_scalar_mul(sclv[:], sclv[:], 128.0), reads=["sclv"], writes=["sclv"])
            E(S, lambda s: s.activation(sgn, xnq, AF.Sign), reads=[xk], writes=["sgn"])
            E(V, lambda v: v.tensor_scalar_mul(sgn, sgn, KofsB[:, 0:1]),
              reads=["sgn", "KofsB"], writes=["sgn"])
            E(V, lambda v: v.scalar_tensor_tensor(xnq, xnq, sclv[:, 0:1], sgn, ALU.mult, ALU.add),
              reads=[xk, "sclv", "sgn"], writes=[xk], tag="qz")
            for hq in range(2):
                E(V, lambda v, hq=hq: v.tensor_copy(qint, xnq[:, hq * (D // 2):(hq + 1) * (D // 2)]),
                  reads=[xk], writes=["qint"])
                E(V, lambda v, hq=hq: v.tensor_scalar(qbf[:, hq * (D // 2):(hq + 1) * (D // 2)], qint,
                                                      127.0, None, ALU.min),
                  reads=["qint"], writes=["qbf"], tag="qmin")
            if d_iq is not None:
                DMA("sync", 25, d_iq[i], qbf, reads=["qbf"])
            transpose16("qbf", qbf, "iaT", iaT[:, :, i * 128:(i + 1) * 128],
                        DC, f"ia{i}", psT, T_)
        DMA("sync", 21, rowscr.ap()[0].rearrange("t p -> p t"), ainv_all[0:128, 0:TT],
            reads=["ainv_all"], writes=[("rowscr", 0)], tag="aivst", slow_ok=True)
        DMA("sync", 21, arow, rowscr.ap()[0:1, :, :].rearrange("a t p -> a (t p)"),
            reads=[("rowscr", 0)], writes=["arow"], tag="aivld")
        bcast_row("arow", arow, arow16, "a16", arowlo, "alo", "aivB", aivB, S_own, (psC, C_), "aiv")

        em.fence()
        AFa.set_base(2 * S_own)
        wf_slab2 = AFa.alloc(4 * D, (4, D))
        wsgn1 = AFa.alloc(2 * D, (2, D))
        wsgn2 = AFa.alloc(2 * D, (2, D))
        gate_s = AFa.alloc(S_own)
        up_s = AFa.alloc(S_own)
        sg_s = AFa.alloc(S_own)
        y1st = AFa.alloc(2 * S_own, (2, S_own))
        d_gate = dbg("gate", [1, 128, S_own])
        d_y1 = dbg("y1", [2, 128, S_own])
        def ldslab(oc):
            for mi, Wap in enumerate((Wg, Wu)):
                sl4 = 2 * mi + oc % 2
                DMA("sync", 4 + sl4, wf_slab2[:, sl4, :],
                    Wap.rearrange("(s p) i -> s p i", p=128)[oc], writes=[("wfb", sl4)],
                    tag=f"ldgu{oc}_{mi}")
        def tern_oc(oc):
            for mi, mcol in ((0, 0), (1, 1)):
                sl4 = 2 * mi + oc % 2
                ternarize(("wfb", sl4), wf_slab2[:, sl4, :], ("wt_bf", mi), wt_bf[:, mi, :], D, mcol,
                          wsgn1[:, mi, :], ("wsg1", mi), wsgn2[:, mi, :], ("wsg2", mi), f"t{oc}_{mi}")
        ldslab(0)
        if AC > 1:
            ldslab(1)
        tern_oc(0)

        def h1_te(oc):
            for ns in range(NS):
                pd = (psC, C_) if ns % 2 == 0 else (psD, D_)
                E(T, lambda t, ns=ns, pd=pd: t.matmul(pd[0][:, 0:ST], sh128[:],
                                                      hi_s[:, ns * ST:(ns + 1) * ST],
                                                      start=True, stop=False),
                  reads=["sh128", "hi_s"], writes=[pd[1]], tag="h1mm")
                E(T, lambda t, ns=ns, pd=pd: t.matmul(pd[0][:, 0:ST], sh128[:],
                                                      lo_s[:, ns * ST:(ns + 1) * ST],
                                                      start=False, stop=True),
                  reads=["sh128", "lo_s"], writes=[pd[1]], tag="h1mm2")

        def h1_ev(oc):
            for ns in range(NS):
                pd = (psC, C_) if ns % 2 == 0 else (psD, D_)
                E(V, lambda v, ns=ns, pd=pd, oc=oc: v.tensor_copy(y1st[:, oc % 2, ns * ST:(ns + 1) * ST],
                                                                  pd[0][:, 0:ST]),
                  reads=[pd[1]], writes=[("y1st", oc % 2)], tag="y1ev")
            DMA("sync", 12 + oc % 2, y1_scr.ap()[:, oc, :], y1st[:, oc % 2, :],
                reads=[("y1st", oc % 2)], writes=[("y1_scr",)], tag=f"sty1{oc}")
            if d_y1 is not None and oc < 2:
                DMA("sync", 25, d_y1[oc], y1st[:, oc % 2, :], reads=[("y1st", oc % 2)])

        def mm_chain(oc, mi):
            buf = oc % 2
            pdst, pkey = ((psA, A_) if mi == 0 else (psB, B_))
            for c in range(DC):
                for ns in range(NS):
                    E(T, lambda t, c=c, ns=ns, buf=buf, mi=mi, pdst=pdst: t.matmul(
                        pdst[:, ns * ST:(ns + 1) * ST], wtT[:, buf, mi, c, :],
                        iaT[:, c, ns * ST:(ns + 1) * ST], start=(c == 0), stop=(c == DC - 1)),
                      reads=[("wtT", buf, mi), "iaT"], writes=[pkey], tag=f"mm{mi}")

        for oc in range(AC):
            buf = oc % 2
            for mi in (0, 1):
                transpose16(("wt_bf", mi), wt_bf[:, mi, :], ("wtT", buf, mi), wtT[:, buf, mi, :, :],
                            DC, f"wt{oc}_{mi}", psT if mi == 0 else psU, T_ if mi == 0 else U_)
            mm_chain(oc, 0)
            if oc > 0:
                h1_te(oc - 1)
            mm_chain(oc, 1)
            if oc + 1 < AC:
                tern_oc(oc + 1)
            if oc > 0:
                h1_ev(oc - 1)
            if oc + 2 < AC:
                ldslab(oc + 2)
            E(V, lambda v: v.scalar_tensor_tensor(gate_s, psA[:, 0:S_own], mB[:, 0:1],
                                                  aivB, ALU.mult, ALU.mult),
              reads=[A_, "mB", "aivB"], writes=["gate_s"], tag="gev")
            E(S, lambda s: s.activation(sg_s, gate_s, AF.Silu),
              reads=["gate_s"], writes=["sg_s"])
            E(V, lambda v: v.scalar_tensor_tensor(up_s, psB[:, 0:S_own], mB[:, 1:2],
                                                  aivB, ALU.mult, ALU.mult),
              reads=[B_, "mB", "aivB"], writes=["up_s"], tag="uev")
            E(V, lambda v: v.tensor_mul(up_s, up_s, sg_s),
              reads=["up_s", "sg_s"], writes=["up_s"])
            if d_gate is not None and oc == 0:
                DMA("sync", 25, d_gate[0], gate_s, reads=["gate_s"])
            E(V, lambda v: v.tensor_copy(hi_s, up_s), reads=["up_s"], writes=["hi_s"])
            E(V, lambda v: v.scalar_tensor_tensor(lo_s, up_s, 1.0, hi_s, ALU.mult, ALU.subtract),
              reads=["up_s", "hi_s"], writes=["lo_s"])
        h1_te(AC - 1)
        h1_ev(AC - 1)

        em.fence()
        ABa.set_base(AB_P)
        y2b = ABa.alloc(AC * FWIN, (AC, FWIN))
        zrow16 = ABa.alloc(FWIN, rows=1)
        zrowlo = ABa.alloc(FWIN, rows=1)
        AFa.set_base(0)
        y1wD = AFa.alloc(2 * AC * FWIN, (2, AC, FWIN))
        zchD = AFa.alloc(2 * 4 * FWIN, (2, 4, FWIN))
        ymaxD = AFa.alloc(2 * FWIN, (2, FWIN))
        zsBD = AFa.alloc(2 * FWIN, (2, FWIN))
        zrowD = AFa.alloc(2 * FWIN, (2, FWIN), rows=1)
        y2icD = AIa.alloc(8 * FWIN, (2, 4, FWIN), off=0)

        d_scr2 = dbg("scr2", [128, AC, FWIN])
        if d_scr2 is not None:
            DMA("sync", 25, d_scr2, y1_scr.ap()[:, :, 2 * FWIN:3 * FWIN], reads=[("y1_scr",)])
        d_y2 = dbg("y2", [2, 128, FWIN], BF16)
        NBST = int(np.log2(AC))
        NSUBW = FWIN // 128
        def p1c_front(w):
            wp = w % 2
            y1w = y1wD[:, wp, :, :]
            yk1 = ("y1w", wp)
            ymax = ymaxD[:, wp, :]
            ymk = ("ymax", wp)
            for pc in range(4):
                a0, a1 = pc * (AC // 4), (pc + 1) * (AC // 4)
                DMA("sync", 12 + pc % 2, y1w[:, a0:a1, :], y1_scr.ap()[:, a0:a1, w * FWIN:(w + 1) * FWIN],
                    reads=[("y1_scr",)], writes=[yk1], tag=f"ldy1w{w}_{pc}")
            d_y1w = dbg("y1w2", [128, AC, FWIN]) if w == 2 else None
            if d_y1w is not None:
                DMA("sync", 25, d_y1w, y1w, reads=[yk1])
            for st in range(NBST):
                off = 2 ** st
                y4 = y1w.rearrange("p (blk two off) f -> p blk two off f", two=2, off=off)
                A4 = y4[:, :, 0, :, :]
                B4 = y4[:, :, 1, :, :]
                E(V, lambda v, A4=A4, B4=B4: v.tensor_add(A4, A4, B4), reads=[yk1],
                  writes=[yk1], tag=f"bfa{w}_{st}")
                E(V, lambda v, A4=A4, B4=B4: v.scalar_tensor_tensor(B4, B4, -2.0, A4, ALU.mult, ALU.add),
                  reads=[yk1], writes=[yk1], tag=f"bfs{w}_{st}")
            E(V, lambda v, y1w=y1w, ymax=ymax: v.tensor_reduce(ymax, y1w.rearrange("p a f -> p f a"),
                                                                AX.X, ALU.max, apply_absolute_value=True),
              reads=[yk1], writes=[ymk], tag="ymaxa")

        def p1c_tail(w):
            wp = w % 2
            y1w = y1wD[:, wp, :, :]
            yk1 = ("y1w", wp)
            ymax = ymaxD[:, wp, :]
            zsB = zsBD[:, wp, :]
            zrow = zrowD[0:1, wp, :]
            ymk, zbk, zrk = ("ymax", wp), ("zsB", wp), ("zrow", wp)
            for sw in range(NSUBW):
                tt = w * NSUBW + sw
                E(T, lambda t, sw=sw, ymax=ymax: t.transpose(psA[:, 0:128],
                                                             ymax[:, sw * 128:(sw + 1) * 128], sid32[:]),
                  reads=[ymk, "sid32"], writes=[A_], tag="ymtr")
                E(V, lambda v, wp=wp: v.tensor_reduce(tmax[:, wp:wp + 1], psA[:, 0:128], AX.X, ALU.max),
                  reads=[A_], writes=[("tmax", wp)], tag="ymaxb")
                E(V, lambda v, wp=wp: v.tensor_scalar(tmax[:, wp:wp + 1], tmax[:, wp:wp + 1], C_FW, EPS,
                                                      ALU.mult, ALU.max),
                  reads=[("tmax", wp)], writes=[("tmax", wp)])
                E(V, lambda v, tt=tt, wp=wp: v.tensor_scalar_mul(s2i_all[:, tt:tt + 1], tmax[:, wp:wp + 1],
                                                                 1.0 / 128.0),
                  reads=[("tmax", wp)], writes=["s2i_all"])
                E(V, lambda v, wp=wp: v.reciprocal(zs_tm[:, wp:wp + 1], tmax[:, wp:wp + 1]),
                  reads=[("tmax", wp)], writes=[("zs_tm", wp)])
                E(V, lambda v, wp=wp: v.tensor_scalar_mul(zs_tm[:, wp:wp + 1], zs_tm[:, wp:wp + 1],
                                                          128.0 * C_FW),
                  reads=[("zs_tm", wp)], writes=[("zs_tm", wp)])
                E(T, lambda t, wp=wp: t.transpose(psA[0:1, 128:256], zs_tm[:, wp:wp + 1], sid32[:]),
                  reads=[("zs_tm", wp), "sid32"], writes=[A_], tag="zstr")
                E(V, lambda v, sw=sw, zrow=zrow: v.tensor_copy(zrow[0:1, sw * 128:(sw + 1) * 128],
                                                               psA[0:1, 128:256]),
                  reads=[A_], writes=[zrk], tag="zsev")
            bcast_row(zrk, zrow, zrow16, "z16", zrowlo, "zlo", zbk, zsB, FWIN, (psB, B_), f"zs{w}")
            for a4 in range(0, AC, 4):
                gp = (a4 // 4) % 2
                zch = zchD[:, gp, :, :]
                zk = ("zch", gp)
                E(S, lambda s, a4=a4, y1w=y1w, zch=zch: s.activation(zch, y1w[:, a4:a4 + 4, :], AF.Sign),
                  reads=[yk1], writes=[zk], tag="sgn2")
                E(S, lambda s, zch=zch: s.activation(zch, zch, AF.Copy, scale=KofsB[:, 0:1]),
                  reads=[zk, "KofsB"], writes=[zk], tag="ko2")
                zsb4 = zsB.rearrange("p (a f) -> p a f", a=1).broadcast_to([128, 4, FWIN])
                E(V, lambda v, a4=a4, y1w=y1w, zsb4=zsb4: v.tensor_mul(y1w[:, a4:a4 + 4, :],
                                                                       y1w[:, a4:a4 + 4, :], zsb4),
                  reads=[yk1, zbk], writes=[yk1], tag="zmul")
                E(V, lambda v, a4=a4, y1w=y1w, zch=zch, gp=gp: v.scalar_tensor_tensor(
                    y2icD[:, gp, :, :], y1w[:, a4:a4 + 4, :], 127.0,
                    zch, ALU.min, ALU.add),
                  reads=[yk1, zk], writes=[("y2ic", gp)], tag="zqf")
                E(G, lambda g, a4=a4, gp=gp: g.tensor_copy(y2b[:, a4:a4 + 4, :], y2icD[:, gp, :, :]),
                  reads=[("y2ic", gp)], writes=["y2b"], tag="y2min")
            for pc in range(4):
                a0, a1 = pc * (AC // 4), (pc + 1) * (AC // 4)
                DMA("sync", 14 + pc % 2, y2_scr.ap()[:, a0:a1, w * FWIN:(w + 1) * FWIN], y2b[:, a0:a1, :],
                    reads=["y2b"], writes=[("y2_scr",)], tag=f"sty2{w}_{pc}")
            if d_y2 is not None and w == 0:
                DMA("sync", 25, d_y2[0], y2b[:, 0, :], reads=["y2b"])
                DMA("sync", 25, d_y2[1], y2b[:, 1, :], reads=["y2b"])

        p1c_front(0)
        if NW > 1:
            p1c_front(1)
        for w in range(NW):
            p1c_tail(w)
            if w + 2 < NW:
                p1c_front(w + 2)

        em.fence()
        ABa.set_base(AB_P)
        wd_bfq = ABa.alloc(I, (4, I // 4))
        y2rhs = ABa.alloc(3 * 2 * S_own, (3, 2, S_own))
        wdT = ABa.alloc(2 * AC * 128, (2, AC, 128))
        ar16d = ABa.alloc(512, rows=1)
        arlod = ABa.alloc(512, rows=1)
        AFa.set_base(0)
        wd_slab2q = AFa.alloc(I, (4, I // 4))
        wdsg1 = AFa.alloc(I // 4)
        wdsg2 = AFa.alloc(I // 4)
        down_s = AFa.alloc(2 * S_own, (2, S_own))
        gsil = AFa.alloc(S_own)
        gstg = AFa.alloc(2 * 128, (2, 128))
        s2ivB = AFa.alloc(S_own)
        arow2 = AFa.alloc(S_own, rows=1)

        DMA("sync", 21, rowscr.ap()[1].rearrange("t p -> p t"), s2i_all[0:128, 0:TT],
            reads=["s2i_all"], writes=[("rowscr", 1)], tag="s2st", slow_ok=True)
        DMA("sync", 21, arow2, rowscr.ap()[1:2, :, :].rearrange("a t p -> a (t p)"),
            reads=[("rowscr", 1)], writes=["arow2"], tag="s2ld")
        CHb = min(512, S_own)
        for j4 in range(S_own // CHb):
            bcast_row("arow2", arow2[0:1, j4 * CHb:(j4 + 1) * CHb], ar16d, "a16d", arlod, "alod",
                      "s2ivB", s2ivB[:, j4 * CHb:(j4 + 1) * CHb], CHb, (psC, C_), f"s2b{j4}")
        d_s2b = dbg("s2b", [128, S_own])
        if d_s2b is not None:
            DMA("sync", 25, d_s2b, s2ivB, reads=["s2ivB"])
        d_s2a = dbg("s2a", [128, max(TT, 2)])
        if d_s2a is not None:
            DMA("sync", 25, d_s2a, s2i_all[:], reads=["s2i_all"])
        d_ar2 = dbg("ar2", [1, S_own])
        if d_ar2 is not None:
            DMA("sync", 25, d_ar2, arow2, reads=["arow2"])

        d_dn = dbg("dn", [128, S_own])
        odgrps = [list(range(g0, min(g0 + 2, D // 128))) for g0 in range(0, D // 128, 2)]
        QW = I // 4
        QC = QW // 128

        def ldwd(od, q):
            buf = (od % 2) * 2 + q % 2
            DMA("sync", 8 + (2 * (od % 2) + q) % 4, wd_slab2q[:, buf, :],
                Wd.rearrange("(s p) i -> s p i", p=128)[od][:, q * QW:(q + 1) * QW],
                writes=[("wdb", buf)], tag=f"lwd{od}_{q}")

        def tern_unit(od, q):
            buf = (od % 2) * 2 + q % 2
            ternarize(("wdb", buf), wd_slab2q[:, buf, :], ("wdq", buf), wd_bfq[:, buf, :],
                      QW, 2, wdsg1, "wdsg1", wdsg2, "wdsg2", f"d{od}_{q}")

        def transp_unit(od, q):
            j = od % 2
            buf = j * 2 + q % 2
            transpose16(("wdq", buf), wd_bfq[:, buf, :], ("wdT", j, q),
                        wdT[:, j, q * QC:(q + 1) * QC, :],
                        QC, f"wdt{od}_{q}", psT if j == 0 else psU, T_ if j == 0 else U_)

        def prep_group(grp):
            for q in (0, 1):
                for od in grp:
                    ldwd(od, q)
            for q in range(4):
                for od in grp:
                    tern_unit(od, q)
                if q + 2 < 4:
                    for od in grp:
                        ldwd(od, q + 2)
                for od in grp:
                    transp_unit(od, q)

        y2items = [(gi, a0) for gi in range(len(odgrps)) for a0 in range(0, AC, 2)]
        y2ptr = [0]

        def ldy2_next():
            if y2ptr[0] >= len(y2items):
                return
            gi_, a0_ = y2items[y2ptr[0]]
            bi = y2ptr[0] % 3
            y2ptr[0] += 1
            DMA("sync", 14 + bi, y2rhs[:, bi, :, :], y2_scr.ap()[:, a0_:a0_ + 2, :],
                reads=[("y2_scr",)], writes=[("y2r", bi)], tag=f"ly2_{gi_}_{a0_}")

        prep_group(odgrps[0])
        for gi, grp in enumerate(odgrps):
            nxt = odgrps[gi + 1] if gi + 1 < len(odgrps) else None
            acts_pre, acts_post = {}, {}
            if nxt is not None:
                acts_pre.setdefault(0, []).extend(("ld", od, 0) for od in nxt)
                acts_pre.setdefault(0, []).extend(("ld", od, 1) for od in nxt)
                for q in range(4):
                    acts_pre.setdefault(16 * q + 8, []).extend(("tern", od, q) for od in nxt)
                    if q + 2 < 4:
                        acts_pre.setdefault(16 * q + 10, []).extend(("ld", od, q + 2) for od in nxt)
                    acts_post.setdefault(16 * q + 14, []).extend(("tr", od, q) for od in nxt)

            def do(acts):
                for kind, od, q in acts:
                    if kind == "ld":
                        ldwd(od, q)
                    elif kind == "tern":
                        tern_unit(od, q)
                    else:
                        transp_unit(od, q)

            for a0 in range(0, AC, 2):
                gc = gi * (AC // 2) + a0 // 2
                while y2ptr[0] <= gc + 2 and y2ptr[0] < len(y2items):
                    ldy2_next()
                do(acts_pre.get(a0, ()))
                rk = ("y2r", gc % 3)
                for a in range(a0, a0 + 2):
                    for j, od in enumerate(grp):
                        pd = (psA, A_) if j == 0 else (psB, B_)
                        for ns in range(NS):
                            E(T, lambda t, a=a, a0=a0, gc=gc, j=j, ns=ns, pd=pd: t.matmul(
                                pd[0][:, ns * ST:(ns + 1) * ST], wdT[:, j, a, :],
                                y2rhs[:, gc % 3, a - a0, ns * ST:(ns + 1) * ST],
                                start=(a == 0), stop=(a == AC - 1)),
                              reads=[("wdT", j, a // QC), rk], writes=[pd[1]], tag="dmm")
                do(acts_post.get(a0, ()))
            for ns in range(NS):
                for j, od in enumerate(grp):
                    jp = j % 2
                    pd = (psA, A_) if j == 0 else (psB, B_)
                    E(V, lambda v, ns=ns, pd=pd, jp=jp: v.scalar_tensor_tensor(
                        down_s[:, jp, ns * ST:(ns + 1) * ST], pd[0][:, ns * ST:(ns + 1) * ST], mB[:, 2:3],
                        s2ivB[:, ns * ST:(ns + 1) * ST], ALU.mult, ALU.mult),
                      reads=[pd[1], "mB", "s2ivB"], writes=[("down_s", jp)], tag="dev")
                    E(S, lambda s, ns=ns, jp=jp: s.activation(gsil[:, ns * ST:(ns + 1) * ST],
                                                              down_s[:, jp, ns * ST:(ns + 1) * ST], AF.Silu),
                      reads=[("down_s", jp)], writes=["gsil"], tag="dsil")
                    if d_dn is not None and od == 0:
                        DMA("sync", 25, d_dn[:, ns * ST:(ns + 1) * ST],
                            down_s[:, jp, ns * ST:(ns + 1) * ST], reads=[("down_s", jp)])
                    for k in range(ST // 128):
                        tt = (ns * ST) // 128 + k
                        E(T, lambda t, k=k, ns=ns: t.transpose(
                            psD[:, (k % 4) * 128:(k % 4 + 1) * 128],
                            gsil[:, ns * ST + k * 128:ns * ST + (k + 1) * 128], sid32[:]),
                          reads=["gsil", "sid32"], writes=[D_], tag="gtr")
                        E(S, lambda s, k=k, tt=tt: s.copy(gstg[:, tt % 2, :],
                                                          psD[:, (k % 4) * 128:(k % 4 + 1) * 128]),
                          reads=[D_], writes=[("gstg", tt % 2)], tag="gtev")
                        DMA("sync", 18 + tt % 2,
                            g_scr.ap().rearrange("(t p) d -> t p d", p=128)[tt][:, od * 128:(od + 1) * 128],
                            gstg[:, tt % 2, :], reads=[("gstg", tt % 2)], writes=[("g_scr",)],
                            tag=f"stg{od}_{tt}")

        em.fence()
        ABa.set_base(AB_P)
        attnU = ABa.alloc(TA * S_own, (TA, S_own))
        oh = ABa.alloc(TT * H * 132, (TT, H, 132))

        d_oh = dbg("oh", [128, TT, H, 132], BF16)
        if d_oh is not None:
            E(V, lambda v: v.memset(oh, 0.0), writes=["oh"])
        for h in range(H):
            for kc in range(TA):
                pd = (psA, A_) if kc % 2 == 0 else (psB, B_)
                for ns in range(NS):
                    E(T, lambda t, h=h, kc=kc, ns=ns, pd=pd: t.matmul(
                        pd[0][:, ns * ST:(ns + 1) * ST], khT[:, kc * 128:(kc + 1) * 128],
                        qhT[:, h, ns * ST:(ns + 1) * ST], start=True, stop=True),
                      reads=["khT", "qhT"], writes=[pd[1]], tag="smm")
                E(S, lambda s, kc=kc, pd=pd: s.activation(attnU[:, kc, :], pd[0][:, 0:S_own], AF.Exp),
                  reads=[pd[1]], writes=[("aU", kc)], tag="expev")
            for q0 in range(0, TT, 2):
                qs = list(range(q0, min(q0 + 2, TT)))
                for kc in range(TA):
                    for j, q in enumerate(qs):
                        pd = (psC, C_) if j == 0 else (psD, D_)
                        E(T, lambda t, kc=kc, q=q, pd=pd: t.matmul(
                            pd[0][:, 0:129], attnU[:, kc, q * 128:(q + 1) * 128],
                            vT1[:, kc, 0:129], start=(kc == 0), stop=(kc == TA - 1)),
                          reads=[("aU", kc), ("vT1", kc), "vT1ones"], writes=[pd[1]], tag="avmm")
                for j, q in enumerate(qs):
                    pd = (psC, C_) if j == 0 else (psD, D_)
                    E(V, lambda v, pd=pd: v.reciprocal(recips[:], pd[0][:, 128:129]),
                      reads=[pd[1]], writes=["recips"], tag="avrc")
                    E(S, lambda s, pd=pd, q=q, h=h: s.activation(oh[:, q, h, 0:128], pd[0][:, 0:128],
                                                                 AF.Copy, scale=recips[:, 0:1]),
                      reads=[pd[1], "recips"], writes=["oh"], tag="ohev")
        if d_oh is not None:
            DMA("sync", 25, d_oh, oh, reads=["oh"])

        em.fence()
        wq_slab2 = ABa.alloc(2 * DC * 128, (2, DC, 128), off=0)
        ABa.set_base(AB_P)
        ohT = ABa.alloc(DC * S_own, (DC, S_own))
        AFa.set_base(0)
        wo_evD = AFa.alloc(2 * S_own, (2, S_own))
        aostg = AFa.alloc(8 * 128, (8, 128))

        for tt in range(TT):
            for h in range(H):
                jj = (tt * H + h) % 8
                E(T, lambda t, tt=tt, h=h, jj=jj: t.transpose(psT[:, jj * 128:(jj + 1) * 128],
                                                              oh[:, tt, h, 0:128], sid[:]),
                  reads=["oh", "sid"], writes=[T_], tag="ohtr")
                E(V, lambda v, tt=tt, h=h, jj=jj: v.tensor_copy(ohT[:, h, tt * 128:(tt + 1) * 128],
                                                                psT[:, jj * 128:(jj + 1) * 128]),
                  reads=[T_], writes=["ohT"], tag="ohtev")
        wq_stg = AFa.alloc(2 * DC * 128, (2, DC, 128))

        def ldwo(oc):
            DMA("sync", 12 + oc % 2, wq_stg[:, oc % 2, :, :],
                Wout[:, oc * 128:(oc + 1) * 128].rearrange("(c p) m -> p c m", p=128),
                writes=[("wqs", oc % 2)], tag=f"ldwo{oc}")
            E(V, lambda v, oc=oc: v.tensor_copy(wq_slab2[:, oc % 2, :, :], wq_stg[:, oc % 2, :, :]),
              reads=[("wqs", oc % 2)], writes=[("wq", oc % 2)], tag=f"cwo{oc}")
        ldwo(0)
        if DC > 1:
            ldwo(1)
        def ao_out(oc):
            for tt in range(TT):
                E(T, lambda t, tt=tt, oc=oc: t.transpose(psB[:, (tt % 8) * 128:(tt % 8 + 1) * 128],
                                                         wo_evD[:, oc % 2, tt * 128:(tt + 1) * 128],
                                                         sid32[:]),
                  reads=[("wo_ev", oc % 2), "sid32"], writes=[B_], tag="aotr")
                E(S, lambda s, tt=tt: s.copy(aostg[:, tt, :],
                                             psB[:, (tt % 8) * 128:(tt % 8 + 1) * 128]),
                  reads=[B_], writes=[("aostg", tt)], tag="aoev")
                DMA("sync", 20 + tt % 4,
                    ao_scr.ap().rearrange("(t p) d -> t p d", p=128)[tt][:, oc * 128:(oc + 1) * 128],
                    aostg[:, tt, :], reads=[("aostg", tt)], writes=[("ao_scr",)],
                    tag=f"stao{oc}_{tt}")

        for oc in range(DC):
            wk = ("wq", oc % 2)
            for c in range(DC):
                for ns in range(NS):
                    E(T, lambda t, c=c, ns=ns, oc=oc: t.matmul(
                        psA[:, ns * ST:(ns + 1) * ST], wq_slab2[:, oc % 2, c, :],
                        ohT[:, c, ns * ST:(ns + 1) * ST], start=(c == 0), stop=(c == DC - 1)),
                      reads=[wk, "ohT"], writes=[A_], tag="womm")
            E(V, lambda v, oc=oc: v.tensor_copy(wo_evD[:, oc % 2, :], psA[:, 0:S_own]),
              reads=[A_], writes=[("wo_ev", oc % 2)])
            if oc > 0:
                ao_out(oc - 1)
            if oc + 2 < DC:
                ldwo(oc + 2)
        ao_out(DC - 1)

        em.fence()
        AFa.set_base(0)
        aot = AFa.alloc(2 * D, (2, D))
        goB = AFa.alloc(D)
        stileD = AFa.alloc(2 * D, (2, D))
        xrb = AFa.alloc(2 * D, (2, D))
        grb = AFa.alloc(2 * D, (2, D))
        outt = AFa.alloc(2 * D, (2, D))
        DMA("sync", 26, goB, g_outB, writes=["goB"])
        d_ao = dbg("ao", [TT, 128, D])

        def p3_front(tt):
            DMA("sync", 0 + tt % 2, aot[:, tt % 2, :],
                ao_scr.ap().rearrange("(t p) d -> t p d", p=128)[tt],
                reads=[("ao_scr",)], writes=[("aot", tt % 2)], tag=f"lao{tt}")
            DMA("sync", 4 + tt % 2, xrb[:, tt % 2, :],
                x_own.rearrange("(t p) d -> t p d", p=128)[tt], writes=[("xrb", tt % 2)], tag=f"lx2_{tt}")
            DMA("sync", 6 + tt % 2, grb[:, tt % 2, :],
                g_scr.ap().rearrange("(t p) d -> t p d", p=128)[tt], reads=[("g_scr",)],
                writes=[("grb", tt % 2)], tag=f"lg2_{tt}")
            if d_ao is not None:
                DMA("sync", 25, d_ao[tt], aot[:, tt % 2, :], reads=[("aot", tt % 2)])
            ln_stats(("aot", tt % 2), aot[:, tt % 2, :], f"l3_{tt}", tt % 2)

        def p3_tail(tt):
            stile = stileD[:, tt % 2, :]
            sk = ("stile", tt % 2)
            E(V, lambda v, tt=tt, stile=stile: v.tensor_scalar(stile, aot[:, tt % 2, :],
                                                               rstd[:, tt % 2:tt % 2 + 1],
                                                               mbv[:, tt % 2:tt % 2 + 1],
                                                               ALU.mult, ALU.add),
              reads=[("aot", tt % 2), ("rstd", tt % 2), ("mbv", tt % 2)], writes=[sk], tag="ln3ap")
            E(V, lambda v, stile=stile: v.tensor_mul(stile, stile, goB),
              reads=[sk, "goB"], writes=[sk])
            E(V, lambda v, tt=tt, stile=stile: v.tensor_mul(outt[:, tt % 2, :], stile, grb[:, tt % 2, :]),
              reads=[sk, ("grb", tt % 2)], writes=[("outt", tt % 2)], tag="sg")
            E(V, lambda v, tt=tt: v.tensor_add(outt[:, tt % 2, :], outt[:, tt % 2, :], xrb[:, tt % 2, :]),
              reads=[("outt", tt % 2), ("xrb", tt % 2)], writes=[("outt", tt % 2)], tag="resid")
            DMA("sync", 22 + tt % 2, OUT.rearrange("(t p) d -> t p d", p=128)[tt], outt[:, tt % 2, :],
                reads=[("outt", tt % 2)], tag=f"stout{tt}")

        p3_front(0)
        if TT > 1:
            p3_front(1)
        for tt in range(TT):
            p3_tail(tt)
            if tt + 2 < TT:
                p3_front(tt + 2)

        em.barrier("sync")
        block = es.enter_context(nc.Block())
        stats = em.finalize(block)
    return nc, stats, sorted(dbgs.keys())




def _hadamard(n):
    h = np.array([[1.0]], dtype=np.float32)
    while h.shape[0] < n:
        h = np.block([[h, h], [h, -h]])
    return h.astype(np.float32)


def kernel(L0_attn_x, L0_attn_g_norm, L0_attn_Wq, L0_attn_Wkv, L0_attn_Wout,
           L0_attn_g_out, L0_ffn_w_gate, L0_ffn_w_up, L0_ffn_w_down):
    X = np.ascontiguousarray(np.asarray(L0_attn_x, dtype=np.float32))
    B, N, Dm = X.shape
    half = N // 2
    cfg = Cfg(D=Dm, S_own=half, S_all=N, I=L0_ffn_w_gate.shape[0], FWIN=128)
    nc, stats, _ = build_program(cfg)

    consts = {
        "g_normB": np.ascontiguousarray(np.broadcast_to(np.asarray(L0_attn_g_norm, np.float32), (128, Dm))),
        "g_outB": np.ascontiguousarray(np.broadcast_to(np.asarray(L0_attn_g_out, np.float32), (128, Dm))),
        "Wq": np.ascontiguousarray(np.asarray(L0_attn_Wq, np.float32)),
        "Wkv": np.ascontiguousarray(np.asarray(L0_attn_Wkv, np.float32)),
        "Wout": np.ascontiguousarray(np.asarray(L0_attn_Wout, np.float32)),
        "Wg": np.ascontiguousarray(np.asarray(L0_ffn_w_gate, np.float32)),
        "Wu": np.ascontiguousarray(np.asarray(L0_ffn_w_up, np.float32)),
        "Wd": np.ascontiguousarray(np.asarray(L0_ffn_w_down, np.float32)),
        "IDN": np.eye(128, dtype=BF),
        "IDN32": np.eye(128, dtype=np.float32),
        "H128": _hadamard(128).astype(BF),
    }
    in_maps = []
    for c in range(8):
        b, hf = c // 2, c % 2
        m = dict(consts)
        m["x_own"] = np.ascontiguousarray(X[b, hf * half:(hf + 1) * half])
        m["x_oth"] = np.ascontiguousarray(X[b, (1 - hf) * half:(2 - hf) * half])
        in_maps.append(m)
    res = run_bass_kernel_spmd(nc, in_maps, core_ids=list(range(8)))
    out = np.empty((B, N, Dm), dtype=np.float32)
    for c in range(8):
        b, hf = c // 2, c % 2
        out[b, hf * half:(hf + 1) * half] = res.results[c]["OUT"]
    return out


# version 24 (final)
# speedup vs baseline: 1.0092x; 1.0092x over previous
import numpy as np
import ml_dtypes
from contextlib import ExitStack
import concourse.bass as bass
import concourse.mybir as mybir
from concourse.bass_utils import run_bass_kernel_spmd

F32 = mybir.dt.float32
BF16 = mybir.dt.bfloat16
I32 = mybir.dt.int32
AF = mybir.ActivationFunctionType
ALU = mybir.AluOpType
AX = mybir.AxisListType
BF = ml_dtypes.bfloat16

ENGINES = ("tensor", "vector", "scalar", "gpsimd", "sync")
EPS = 1e-5


class _Op:
    __slots__ = ("engine", "fn", "sem", "inc_amt", "need_inc", "waits", "idx", "tag")

    def __init__(self, engine, fn, sem, inc_amt, idx, tag):
        self.engine = engine
        self.fn = fn
        self.sem = sem
        self.inc_amt = inc_amt
        self.need_inc = False
        self.waits = []
        self.idx = idx
        self.tag = tag


class Emitter:

    def __init__(self, nc, es, n_dma_slots=28):
        self.nc = nc
        self.ops = []
        self.eng_sems = {e: es.enter_context(nc.semaphore(f"sem_{e}")) for e in ENGINES}
        self.dma_sems = [es.enter_context(nc.semaphore(f"dma{i}")) for i in range(n_dma_slots)]
        self.dma_slot_last = [None] * n_dma_slots
        self.last_writer = {}
        self.readers = {}
        self.eng_wm = {e: {} for e in ENGINES}
        self.fence_deps = []

    def _mkop(self, engine, fn, sem, inc_amt, tag):
        op = _Op(engine, fn, sem, inc_amt, len(self.ops), tag)
        self.ops.append(op)
        return op

    def _add_dep(self, op, dep):
        if dep is None or dep is op:
            return
        if dep.engine == op.engine == "tensor" and dep.sem is op.sem:
            return
        key = id(dep.sem)
        if self.eng_wm[op.engine].get(key, -1) >= dep.idx:
            return
        dep.need_inc = True
        op.waits.append(dep)
        self.eng_wm[op.engine][key] = dep.idx

    def _track(self, op, reads, writes):
        for d in self.fence_deps:
            self._add_dep(op, d)
        for r in reads:
            self._add_dep(op, self.last_writer.get(r))
        for w in writes:
            self._add_dep(op, self.last_writer.get(w))
            for rd in self.readers.get(w, {}).values():
                self._add_dep(op, rd)
        for r in reads:
            self.readers.setdefault(r, {})[(op.engine, id(op.sem))] = op
        for w in writes:
            self.last_writer[w] = op
            self.readers[w] = {}

    def emit(self, engine, fn, reads=(), writes=(), tag=""):
        op = self._mkop(engine, fn, self.eng_sems[engine], 1, tag)
        self._track(op, reads, writes)
        return op

    def dma(self, qengine, slot, out, in_, reads=(), writes=(), tag="", slow_ok=False):
        def fn(eng, _o=out, _i=in_, _s=slow_ok):
            return eng.dma_start(out=_o, in_=_i, allow_slow_non_contiguous=_s)

        op = self._mkop(qengine, fn, self.dma_sems[slot], 16, tag)
        op.need_inc = True
        prev = self.dma_slot_last[slot]
        if prev is not None:
            self._add_dep(op, prev)
        self._track(op, reads, writes)
        self.dma_slot_last[slot] = op
        return op

    def fence(self):
        fl = []
        for e in ENGINES:
            for o in reversed(self.ops):
                if o.engine == e:
                    fl.append(o)
                    break
        fl += [p for p in self.dma_slot_last if p is not None]
        self.fence_deps = fl
        self.last_writer.clear()
        self.readers.clear()

    def barrier(self, engine="sync"):
        op = self._mkop(engine, lambda eng: None, self.eng_sems[engine], 0, "barrier")
        for prev in self.dma_slot_last:
            if prev is not None:
                self._add_dep(op, prev)
        for e in ENGINES:
            if e == engine:
                continue
            for o in reversed(self.ops):
                if o.engine == e:
                    self._add_dep(op, o)
                    break
        return op

    def finalize(self, block):
        sem_count = {}
        op_count = {}
        for op in self.ops:
            if op.need_inc:
                k = id(op.sem)
                sem_count[k] = sem_count.get(k, 0) + op.inc_amt
                op_count[op] = sem_count[k]
        per_engine = {e: [] for e in ENGINES}
        for op in self.ops:
            per_engine[op.engine].append(op)

        def mk(ops):
            def run(eng):
                for op in ops:
                    for dep in op.waits:
                        eng.wait_ge(dep.sem, op_count[dep])
                    inst = op.fn(eng)
                    if op.need_inc:
                        if inst is None:
                            raise RuntimeError(f"op {op.tag} needs inc, no instruction")
                        inst.then_inc(op.sem, op.inc_amt)
            return run

        for e in ENGINES:
            if per_engine[e]:
                getattr(block, e)(mk(per_engine[e]))
        return {"ops": len(self.ops),
                "waits": sum(len(o.waits) for o in self.ops),
                "incs": sum(1 for o in self.ops if o.need_inc)}


class Arena:

    def __init__(self, nc, es, name, dt, n_elems):
        self.t = es.enter_context(nc.sbuf_tensor(name, [128, n_elems], dt))
        self.n = n_elems
        self.cur = 0

    def set_base(self, base):
        self.cur = base

    def alloc(self, elems, shape=None, rows=128, off=None):
        if off is None:
            off = self.cur
            self.cur += elems
        assert off + elems <= self.n, f"arena {self.t.name} overflow {off}+{elems}>{self.n}"
        ap = self.t[0:rows, off:off + elems]
        if shape is not None:
            pat = "p (" + " ".join(f"d{i}" for i in range(len(shape))) + ") -> p " + \
                  " ".join(f"d{i}" for i in range(len(shape)))
            kw = {f"d{i}": s for i, s in enumerate(shape)}
            ap = ap.rearrange(pat, **kw)
        return ap


class Cfg:
    def __init__(self, D=2048, S_own=1024, S_all=2048, I=8192, FWIN=128, DBG=()):
        self.D, self.S_own, self.S_all, self.I, self.FWIN = D, S_own, S_all, I, FWIN
        self.DC = D // 128
        self.H = D // 128
        self.TT = S_own // 128
        self.TA = S_all // 128
        self.AC = I // 128
        self.NW = S_own // FWIN
        self.DBG = set(DBG)
        assert self.AC & (self.AC - 1) == 0 and self.NW >= 1


def build_program(cfg: Cfg):
    D, DC, H = cfg.D, cfg.DC, cfg.H
    S_own, S_all, TT, TA = cfg.S_own, cfg.S_all, cfg.TT, cfg.TA
    I, AC = cfg.I, cfg.AC
    FWIN, NW = cfg.FWIN, cfg.NW
    ST = min(512, S_own)
    NS = S_own // ST
    C_FW = float(np.float32(1.0) / np.float32(np.sqrt(np.float32(I))))
    NBN = max(D // 512, 1)
    WBN = min(D, 512)

    nc = bass.Bass("TRN2", target_bir_lowering=False)

    def din(name, shape, dt=F32):
        return nc.dram_tensor(name, list(shape), dt, kind="ExternalInput").ap()

    x_own = din("x_own", [S_own, D])
    x_oth = din("x_oth", [S_all - S_own, D])
    g_normB = din("g_normB", [128, D])
    g_outB = din("g_outB", [128, D])
    Wq = din("Wq", [D, D])
    Wkv = din("Wkv", [D, 256])
    Wout = din("Wout", [D, D])
    Wg = din("Wg", [I, D])
    Wu = din("Wu", [I, D])
    Wd = din("Wd", [D, I])
    IDN = din("IDN", [128, 128], BF16)
    IDN32 = din("IDN32", [128, 128], F32)
    H128 = din("H128", [128, 128], BF16)

    OUT = nc.dram_tensor("OUT", [S_own, D], F32, kind="ExternalOutput").ap()
    dbgs = {}

    def dbg(name, shape, dt=F32):
        if name in cfg.DBG:
            dbgs[name] = nc.dram_tensor("dbg_" + name, list(shape), dt,
                                        kind="ExternalOutput").ap()
            return dbgs[name]
        return None

    xn_scr = nc.dram_tensor("xn_scr", [S_own, D], F32)
    y1_scr = nc.dram_tensor("y1_scr", [128, AC, S_own], F32)
    y2_scr = nc.dram_tensor("y2_scr", [128, AC, S_own], BF16)
    g_scr = nc.dram_tensor("g_scr", [S_own, D], F32)
    ao_scr = nc.dram_tensor("ao_scr", [S_own, D], F32)
    rowscr = nc.dram_tensor("rowscr", [2, TT, 128], F32)

    es = ExitStack()
    with es:
        em = Emitter(nc, es, n_dma_slots=28)
        sbt = lambda nm, shp, dt: es.enter_context(nc.sbuf_tensor(nm, shp, dt))
        pst = lambda nm, shp, dt: es.enter_context(nc.psum_tensor(nm, shp, dt))
        E, DMA = em.emit, em.dma
        V, S, T, G = "vector", "scalar", "tensor", "gpsimd"

        sid = sbt("sid", [128, 128], BF16)
        sid32 = sbt("sid32", [128, 128], F32)
        sh128 = sbt("sh128", [128, 128], BF16)
        khT = sbt("khT", [128, S_all], BF16)
        vT1 = sbt("vT1", [128, TA, 132], BF16)
        stats6 = sbt("stats6", [128, 2, NBN, 6], F32)
        stats2 = sbt("stats2", [128, 2, 2], F32)
        rstd = sbt("rstd", [128, 2], F32)
        sqv = sbt("sqv", [128, 2], F32)
        mbv = sbt("mbv", [128, 2], F32)
        epsB = sbt("epsB", [128, 1], F32)
        KofsB = sbt("KofsB", [128, 1], F32)
        ktrial = sbt("ktrial", [128, 1], F32)
        ktriali = sbt("ktriali", [128, 1], I32)
        negh = sbt("negh", [128, 1], F32)
        posh = sbt("posh", [128, 1], F32)
        onesc = sbt("onesc", [128, 1], BF16)
        ones_r = sbt("ones_r", [1, 128], BF16)
        amaxv = sbt("amaxv", [128, 1], F32)
        sclv = sbt("sclv", [128, 1], F32)
        recips = sbt("recips", [128, 1], F32)
        tmax = sbt("tmax", [128, 2], F32)
        zs_tm = sbt("zs_tm", [128, 2], F32)
        wabs = sbt("wabs", [128, 1], F32)
        wsum = sbt("wsum", [128, 4], F32)
        wsum16 = sbt("wsum16", [128, 1], BF16)
        wsumlo = sbt("wsumlo", [128, 1], BF16)
        mrow = sbt("mrow", [1, 4], F32)
        mrow16 = sbt("mrow16", [1, 4], BF16)
        mrowlo = sbt("mrowlo", [1, 4], BF16)
        mB = sbt("mB", [128, 4], F32)
        rmB = sbt("rmB", [128, 4], F32)
        ainv_all = sbt("ainv_all", [128, max(TT, 2)], F32)
        s2i_all = sbt("s2i_all", [128, max(TT, 2)], F32)

        AFN = max(
            6 * D + D + 3 * S_all + 512,
            5 * S_own + 2 * ST + 2 * S_own + 4 * D + D + 2 * DC * 128,
            2 * S_own + 4 * D + 4 * D + 5 * S_own,
            2 * AC * FWIN + 8 * FWIN + 6 * FWIN,
            I + I + 3 * S_own + 256 + 2 * S_own,
            11 * D,
            5 * S_own + 2 * ST + 2 * S_own)
        ABN = H * S_own + max(
            TA * S_own + TT * H * 132,
            DC * S_own + 2 * D + 4 * DC * 128 + 2 * S_own + 2 * S_own,
            I + 6 * S_own + 2 * AC * 128 + 1024,
            DC * S_own + 2 * D + DC * 128 + DC * 256 + S_all + 2048,
            DC * S_own + 2 * DC * 128 + 4 * S_own + 2 * ST)
        AIN = max(D // 2, 8 * FWIN)
        AFa = Arena(nc, es, "AF", F32, AFN)
        ABa = Arena(nc, es, "AB", BF16, ABN)
        AIa = Arena(nc, es, "AI", I32, AIN)
        qhT = ABa.alloc(H * S_own, (H, S_own), off=0)
        AB_P = H * S_own

        psA = pst("psA", [128, 1024], F32)
        psB = pst("psB", [128, 1024], F32)
        psC = pst("psC", [128, 512], F32)
        psD = pst("psD", [128, 512], F32)
        psT = pst("psT", [128, 1024], BF16)
        psU = pst("psU", [128, 1024], BF16)
        A_, B_, C_, D_, T_, U_ = "psA", "psB", "psC", "psD", "psT", "psU"

        DMA("sync", 26, sid[:], IDN, writes=["sid"])
        DMA("sync", 26, sid32[:], IDN32, writes=["sid32"])
        DMA("sync", 26, sh128[:], H128, writes=["sh128"])
        E(V, lambda v: v.memset(epsB[:], EPS), writes=["epsB"])
        E(V, lambda v: v.memset(onesc[:], 1.0), writes=["onesc"])
        E(V, lambda v: v.memset(ones_r[:], 1.0), writes=["ones_r"])
        E(V, lambda v: v.memset(negh[:], -0.5), writes=["negh"])
        E(V, lambda v: v.memset(posh[:], 0.5), writes=["posh"])
        E(V, lambda v: v.memset(vT1[:], 0.0), writes=[("vT1", -1)])
        E(V, lambda v: v.memset(vT1[:, :, 128:129], 1.0), reads=[("vT1", -1)], writes=["vT1ones"])
        E(V, lambda v: v.memset(ktrial[:], 0.6), writes=["ktrial"])
        E(V, lambda v: v.tensor_copy(ktriali[:], ktrial[:]), reads=["ktrial"], writes=["ktriali"])
        E(V, lambda v: v.tensor_scalar(KofsB[:], ktriali[:], -0.5, 0.5, ALU.mult, ALU.add),
          reads=["ktriali"], writes=["KofsB"])

        def ln_stats(src_key, src_ap, tag, par=0):
            for c in range(NBN):
                E(V, lambda v, c=c: v.bn_stats(stats6[:, par, c, :], src_ap[:, c * WBN:(c + 1) * WBN]),
                  reads=[src_key], writes=[("st6", par, c)], tag=f"bnst{tag}")
            E(V, lambda v: v.bn_aggr(stats2[:, par, :], stats6[:, par, 0:NBN, :]),
              reads=[("st6", par, c) for c in range(NBN)], writes=[("stats2", par)], tag=f"bnag{tag}")
            E(S, lambda s: s.activation(sqv[:, par:par + 1], stats2[:, par, 1:2], AF.Sqrt,
                                        bias=epsB[:, 0:1]),
              reads=[("stats2", par), "epsB"], writes=[("sqv", par)], tag=f"sq{tag}")
            E(V, lambda v: v.reciprocal(rstd[:, par:par + 1], sqv[:, par:par + 1]),
              reads=[("sqv", par)], writes=[("rstd", par)])
            E(V, lambda v: v.scalar_tensor_tensor(mbv[:, par:par + 1], stats2[:, par, 0:1], -1.0,
                                                  rstd[:, par:par + 1], ALU.mult, ALU.mult),
              reads=[("stats2", par), ("rstd", par)], writes=[("mbv", par)], tag=f"mb{tag}")

        def transpose16(src_key, src_ap, dst_key, dst_ap3, nchunks, tag, ps, pk):
            done = 0
            while done < nchunks:
                nb = min(8, nchunks - done)
                for j in range(nb):
                    c = done + j
                    E(T, lambda t, c=c, j=j: t.transpose(ps[:, j * 128:(j + 1) * 128],
                                                         src_ap[:, c * 128:(c + 1) * 128], sid[:]),
                      reads=[src_key, "sid"], writes=[pk], tag=f"tr{tag}")
                E(V, lambda v, done=done, nb=nb: v.tensor_copy(dst_ap3[:, done:done + nb, :],
                                                               ps[:, 0:nb * 128]),
                  reads=[pk], writes=[dst_key], tag=f"trev{tag}")
                done += nb

        def bcast_row(row_key, row_ap, r16, r16k, rlo, rlok, dst_key, dst_ap, Wn, psel, tag):
            E(V, lambda v: v.tensor_copy(r16[0:1, 0:Wn], row_ap), reads=[row_key], writes=[r16k],
              tag=f"b16{tag}")
            E(V, lambda v: v.scalar_tensor_tensor(rlo[0:1, 0:Wn], row_ap, 1.0, r16[0:1, 0:Wn],
                                                  ALU.mult, ALU.subtract),
              reads=[row_key, r16k], writes=[rlok], tag=f"blo{tag}")
            ps, pk = psel
            for j in range((Wn + 511) // 512):
                w0, w1 = j * 512, min(Wn, (j + 1) * 512)
                E(T, lambda t, w0=w0, w1=w1: t.matmul(ps[:, 0:w1 - w0], ones_r[:], r16[0:1, w0:w1],
                                                      start=True, stop=False),
                  reads=["ones_r", r16k], writes=[pk], tag=f"bmm{tag}")
                E(T, lambda t, w0=w0, w1=w1: t.matmul(ps[:, 0:w1 - w0], ones_r[:], rlo[0:1, w0:w1],
                                                      start=False, stop=True),
                  reads=["ones_r", rlok], writes=[pk], tag=f"bmm2{tag}")
                E(V, lambda v, w0=w0, w1=w1: v.tensor_copy(dst_ap[:, w0:w1], ps[:, 0:w1 - w0]),
                  reads=[pk], writes=[dst_key], tag=f"bev{tag}")

        def ternarize(src_key, src_ap, dst_key, dst_ap, Wn, mcol, g1, g1k, g2, g2k, tag):
            E(S, lambda s: s.activation(g1[:, 0:Wn], src_ap, AF.Sign, bias=negh[:, 0:1],
                                        scale=rmB[:, mcol:mcol + 1]),
              reads=[src_key, "negh", "rmB"], writes=[g1k], tag=f"tg1{tag}")
            E(S, lambda s: s.activation(g2[:, 0:Wn], src_ap, AF.Sign, bias=posh[:, 0:1],
                                        scale=rmB[:, mcol:mcol + 1]),
              reads=[src_key, "posh", "rmB"], writes=[g2k], tag=f"tg2{tag}")
            E(V, lambda v: v.tensor_add(g1[:, 0:Wn], g1[:, 0:Wn], g2[:, 0:Wn]),
              reads=[g1k, g2k], writes=[g1k], tag=f"tadd{tag}")
            E(V, lambda v: v.tensor_scalar_mul(dst_ap, g1[:, 0:Wn], 0.5),
              reads=[g1k], writes=[dst_key], tag=f"thalf{tag}")

        ABa.set_base(AB_P)
        xaT = ABa.alloc(DC * S_own, (DC, S_own))
        xab = ABa.alloc(2 * D, (2, D))
        xoT = ABa.alloc(DC * 128, (DC, 128))
        wkvT = ABa.alloc(DC * 256, (DC, 256))
        vball = ABa.alloc(S_all)
        ksq16a = ABa.alloc(512)
        ksqloa = ABa.alloc(512)
        krr16 = ABa.alloc(512, rows=1)
        krrlo = ABa.alloc(512, rows=1)
        AFa.set_base(0)
        xtile = AFa.alloc(2 * D, (2, D))
        xn = AFa.alloc(2 * D, (2, D))
        xa = AFa.alloc(2 * D, (2, D))
        gnB = AFa.alloc(D)
        kall = AFa.alloc(S_all)
        ksqall = AFa.alloc(S_all)
        rbckall = AFa.alloc(S_all)
        rrk = AFa.alloc(512, rows=1)

        DMA("sync", 26, gnB, g_normB, writes=["gnB"])
        DMA("gpsimd", 24, wkvT, Wkv.rearrange("(c p) n -> p c n", p=128), writes=["wkvT"])

        d_xn = dbg("xn", [TT, 128, D])
        d_xa = dbg("xa", [TA, 128, D])
        d_kh = dbg("kh", [128, S_all], BF16)
        d_vt = dbg("vt", [128, TA, 132], BF16)
        d_qh = dbg("qh", [128, H, S_own], BF16)

        def ldx(i):
            own_ = i < TT
            s_ = (x_own if own_ else x_oth).rearrange("(t p) d -> t p d", p=128)[i if own_ else i - TT]
            DMA("sync", i % 2, xtile[:, i % 2, :], s_, writes=[("xtile", i % 2)], tag=f"ldx{i}")
        ldx(0)
        if TA > 1:
            ldx(1)
        for i in range(TA):
            own = i < TT
            par = i % 2
            xk = ("xtile", par)
            xt_ap = xtile[:, par, :]
            ln_stats(xk, xt_ap, f"l1_{i}", par)
            E(V, lambda v, xt_ap=xt_ap, par=par: v.tensor_scalar(xn[:, par, :], xt_ap, rstd[:, par:par + 1],
                                                                 mbv[:, par:par + 1], ALU.mult, ALU.add),
              reads=[xk, ("rstd", par), ("mbv", par)], writes=[("xn", par)], tag="ln1ap")
            if own:
                DMA("sync", 10 + par, xn_scr.ap().rearrange("(t p) d -> t p d", p=128)[i], xn[:, par, :],
                    reads=[("xn", par)], writes=[("xn_scr",)], tag=f"stxn{i}")
                if d_xn is not None:
                    DMA("sync", 25, d_xn[i], xn[:, par, :], reads=[("xn", par)])
            ln_stats(("xn", par), xn[:, par, :], f"l2_{i}", par)
            E(V, lambda v, par=par: v.tensor_scalar(xa[:, par, :], xn[:, par, :], rstd[:, par:par + 1],
                                                    mbv[:, par:par + 1], ALU.mult, ALU.add),
              reads=[("xn", par), ("rstd", par), ("mbv", par)], writes=[("xa", par)], tag="ln2ap")
            E(V, lambda v, par=par: v.tensor_mul(xa[:, par, :], xa[:, par, :], gnB),
              reads=[("xa", par), "gnB"], writes=[("xa", par)])
            E(V, lambda v, par=par: v.tensor_copy(xab[:, par, :], xa[:, par, :]),
              reads=[("xa", par)], writes=[("xab", par)])
            if d_xa is not None:
                DMA("sync", 25, d_xa[i], xa[:, par, :], reads=[("xa", par)])
            if own:
                transpose16(("xab", par), xab[:, par, :], "xaT",
                            xaT[:, :, i * 128:(i + 1) * 128], DC, f"xa{i}", psU, U_)
                rk, rget = "xaT", (lambda c, i=i: xaT[:, c, i * 128:(i + 1) * 128])
            else:
                transpose16(("xab", par), xab[:, par, :], "xoT", xoT[:, :, :], DC, f"xo{i}", psU, U_)
                rk, rget = "xoT", (lambda c: xoT[:, c, :])
            for half in range(2):
                for c in range(DC):
                    E(T, lambda t, c=c, half=half, rget=rget: t.matmul(
                        psC[:, half * 128:(half + 1) * 128], wkvT[:, c, half * 128:(half + 1) * 128],
                        rget(c), start=(c == 0), stop=(c == DC - 1)),
                      reads=["wkvT", rk], writes=[C_], tag="kvmm")
            E(V, lambda v, i=i: v.tensor_copy(kall[:, i * 128:(i + 1) * 128], psC[:, 0:128]),
              reads=[C_], writes=["kall"], tag="kev")
            E(V, lambda v, i=i: v.tensor_copy(vball[:, i * 128:(i + 1) * 128], psC[:, 128:256]),
              reads=[C_], writes=["vball"], tag="vev0")
            if i + 2 < TA:
                ldx(i + 2)
        E(S, lambda s: s.activation(ksqall, kall, AF.Square), reads=["kall"], writes=["ksqall"])
        for j4 in range(S_all // 512):
            E(V, lambda v, j4=j4: v.tensor_copy(ksq16a, ksqall[:, j4 * 512:(j4 + 1) * 512]),
              reads=["ksqall"], writes=["ksq16a"])
            E(V, lambda v, j4=j4: v.scalar_tensor_tensor(ksqloa, ksqall[:, j4 * 512:(j4 + 1) * 512], 1.0,
                                                         ksq16a, ALU.mult, ALU.subtract),
              reads=["ksqall", "ksq16a"], writes=["ksqloa"])
            E(T, lambda t: t.matmul(psD[0:1, 0:512], onesc[:], ksq16a, start=True, stop=False),
              reads=["onesc", "ksq16a"], writes=[D_], tag="ksmm")
            E(T, lambda t: t.matmul(psD[0:1, 0:512], onesc[:], ksqloa, start=False, stop=True),
              reads=["onesc", "ksqloa"], writes=[D_], tag="ksmm2")
            E(V, lambda v, j4=j4: v.tensor_scalar(rrk, psD[0:1, 0:512], 1.0, 1e-12, ALU.mult, ALU.add),
              reads=[D_], writes=["rrk"], tag="kssft")
            E(S, lambda s: s.activation(rrk, rrk, AF.Sqrt), reads=["rrk"], writes=["rrk"])
            E(V, lambda v: v.reciprocal(rrk, rrk), reads=["rrk"], writes=["rrk"])
            E(V, lambda v: v.tensor_scalar_mul(rrk, rrk, 4.0), reads=["rrk"], writes=["rrk"])
            bcast_row("rrk", rrk, krr16, "krr16", krrlo, "krrlo",
                      "rbckall", rbckall[:, j4 * 512:(j4 + 1) * 512], 512, (psD, D_), f"kb{j4}")
        E(V, lambda v: v.tensor_mul(khT[:], kall, rbckall), reads=["kall", "rbckall"], writes=["khT"],
          tag="khmul")
        for i0 in range(0, TA, 8):
            nb = min(8, TA - i0)
            for j in range(nb):
                E(T, lambda t, i0=i0, j=j: t.transpose(psT[:, j * 128:(j + 1) * 128],
                                                       vball[:, (i0 + j) * 128:(i0 + j + 1) * 128], sid[:]),
                  reads=["vball", "sid"], writes=[T_], tag="vtr")
            E(V, lambda v, i0=i0, nb=nb: v.tensor_copy(vT1[:, i0:i0 + nb, 0:128], psT[:, 0:nb * 128]),
              reads=[T_, ("vT1", -1)], writes=[("vT1", i) for i in range(i0, i0 + nb)], tag="vev")
        if d_kh is not None:
            DMA("sync", 25, d_kh, khT[:], reads=["khT"])
        if d_vt is not None:
            DMA("sync", 25, d_vt, vT1[:], reads=[("vT1", i) for i in range(TA)] + ["vT1ones"])

        em.fence()
        ABa.set_base(AB_P)
        xaT2 = ABa.alloc(DC * S_own, (DC, S_own))
        wq_slab = ABa.alloc(2 * DC * 128, (2, DC, 128))
        qsq16 = ABa.alloc(2 * S_own, (2, S_own))
        qsqlo = ABa.alloc(2 * S_own, (2, S_own))
        rr16q = ABa.alloc(ST, rows=1)
        rrloq = ABa.alloc(ST, rows=1)
        AFa.set_base(0)
        qraw = AFa.alloc(3 * S_own, (3, S_own))
        qsq = AFa.alloc(2 * S_own, (2, S_own))
        rbc = AFa.alloc(2 * ST, (2, ST))
        rrow = AFa.alloc(2 * S_own, (2, S_own), rows=1)

        wf_slab = AFa.alloc(4 * D, (4, D))
        dumpb = AFa.alloc(D)
        wq_stg0 = AFa.alloc(2 * DC * 128, (2, DC, 128))
        mean_items = []
        for col, Wap, n_el in ((0, Wg, float(I * D)), (1, Wu, float(I * D)), (2, Wd, float(D * I))):
            if Wap is Wd:
                for si in range(D // 128):
                    for q in range(I // D):
                        mean_items.append((col, Wap.rearrange("(s p) i -> s p i", p=128)[si]
                                           [:, q * D:(q + 1) * D]))
            else:
                for si in range(I // 128):
                    mean_items.append((col, Wap.rearrange("(s p) i -> s p i", p=128)[si]))
        _mi_counter = [0]

        def emit_mean(n):
            for _ in range(n):
                k = _mi_counter[0]
                if k >= len(mean_items):
                    return
                _mi_counter[0] += 1
                col, ap_ = mean_items[k]
                sl = k % 4
                DMA("sync", 4 + sl, wf_slab[:, sl, :], ap_, writes=[("wfb", sl)], tag=f"ldm{k}")
                E(S, lambda s, sl=sl: s.activation(dumpb, wf_slab[:, sl, :], AF.Abs,
                                                   accum_out=wabs[:, 0:1]),
                  reads=[("wfb", sl), "dumpb"], writes=["dumpb", "wabs"], tag="mabs")
                E(V, lambda v, col=col: v.tensor_add(wsum[:, col:col + 1], wsum[:, col:col + 1],
                                                     wabs[:, 0:1]),
                  reads=["wsum", "wabs"], writes=["wsum"], tag="macc")

        E(V, lambda v: v.memset(wsum[:], 0.0), writes=["wsum"])

        def ldwq(h):
            DMA("sync", 8 + h % 2, wq_stg0[:, h % 2, :, :],
                Wq[:, h * 128:(h + 1) * 128].rearrange("(c p) m -> p c m", p=128),
                writes=[("wqs", h % 2)], tag=f"ldwq{h}")
            E(V, lambda v, h=h: v.tensor_copy(wq_slab[:, h % 2, :, :], wq_stg0[:, h % 2, :, :]),
              reads=[("wqs", h % 2)], writes=[("wq", h % 2)], tag=f"cwq{h}")
        ldwq(0)
        if H > 1:
            ldwq(1)
        def stage_a(h):
            hp3, hp = h % 3, h % 2
            wk = ("wq", hp)
            for c in range(DC):
                for ns in range(NS):
                    E(T, lambda t, c=c, ns=ns, hp=hp: t.matmul(
                        psA[:, ns * ST:(ns + 1) * ST], wq_slab[:, hp, c, :],
                        xaT2[:, c, ns * ST:(ns + 1) * ST], start=(c == 0), stop=(c == DC - 1)),
                      reads=[wk, "xaT"], writes=[A_], tag="qmm")
            E(V, lambda v, hp3=hp3: v.tensor_copy(qraw[:, hp3, :], psA[:, 0:S_own]),
              reads=[A_], writes=[("qraw", hp3)])
            E(S, lambda s, hp3=hp3, hp=hp: s.activation(qsq[:, hp, :], qraw[:, hp3, :], AF.Square),
              reads=[("qraw", hp3)], writes=[("qsq", hp)])
            E(V, lambda v, hp=hp: v.tensor_copy(qsq16[:, hp, :], qsq[:, hp, :]),
              reads=[("qsq", hp)], writes=[("qsq16", hp)])
            E(V, lambda v, hp=hp: v.scalar_tensor_tensor(qsqlo[:, hp, :], qsq[:, hp, :], 1.0,
                                                         qsq16[:, hp, :], ALU.mult, ALU.subtract),
              reads=[("qsq", hp), ("qsq16", hp)], writes=[("qsqlo", hp)])

        def stage_b(h):
            hp = h % 2
            for ns in range(NS):
                E(T, lambda t, ns=ns, hp=hp: t.matmul(psD[0:1, 0:ST], onesc[:],
                                                      qsq16[:, hp, ns * ST:(ns + 1) * ST],
                                                      start=True, stop=False),
                  reads=["onesc", ("qsq16", hp)], writes=[D_], tag="qsmm")
                E(T, lambda t, ns=ns, hp=hp: t.matmul(psD[0:1, 0:ST], onesc[:],
                                                      qsqlo[:, hp, ns * ST:(ns + 1) * ST],
                                                      start=False, stop=True),
                  reads=["onesc", ("qsqlo", hp)], writes=[D_], tag="qsmm2")
                E(V, lambda v, ns=ns, hp=hp: v.tensor_scalar(rrow[0:1, hp, ns * ST:(ns + 1) * ST],
                                                             psD[0:1, 0:ST], 256.0, 1e-12,
                                                             ALU.mult, ALU.add),
                  reads=[D_], writes=[("rrow", hp)], tag="qsc")
            E(S, lambda s, hp=hp: s.activation(rrow[0:1, hp, :], rrow[0:1, hp, :], AF.Sqrt),
              reads=[("rrow", hp)], writes=[("rrow", hp)])
            E(V, lambda v, hp=hp: v.reciprocal(rrow[0:1, hp, :], rrow[0:1, hp, :]),
              reads=[("rrow", hp)], writes=[("rrow", hp)])
            E(V, lambda v, hp=hp: v.tensor_scalar_mul(rrow[0:1, hp, :], rrow[0:1, hp, :], 64.0),
              reads=[("rrow", hp)], writes=[("rrow", hp)])

        def stage_c(h):
            hp3, hp = h % 3, h % 2
            for ns in range(NS):
                bcast_row(("rrow", hp), rrow[0:1, hp, ns * ST:(ns + 1) * ST], rr16q, "rr16", rrloq, "rrlo",
                          ("rbc", hp), rbc[:, hp, :], ST, (psC, C_), f"qb{h}_{ns}")
                E(V, lambda v, h=h, ns=ns, hp3=hp3, hp=hp: v.tensor_mul(qhT[:, h, ns * ST:(ns + 1) * ST],
                                                                        qraw[:, hp3, ns * ST:(ns + 1) * ST],
                                                                        rbc[:, hp, 0:ST]),
                  reads=[("qraw", hp3), ("rbc", hp)], writes=["qhT"], tag="qhmul")

        for h in range(H):
            stage_a(h)
            if h >= 1:
                stage_b(h - 1)
            if h >= 2:
                stage_c(h - 2)
            if h + 2 < H:
                ldwq(h + 2)
            emit_mean(len(mean_items) // H + 1)
        stage_b(H - 1)
        if H >= 2:
            stage_c(H - 2)
        stage_c(H - 1)
        if d_qh is not None:
            DMA("sync", 25, d_qh, qhT[:, :, :], reads=["qhT"])

        emit_mean(len(mean_items))

        def mean_finish(col, n_el, tag):
            E(V, lambda v, col=col: v.tensor_copy(wsum16[:], wsum[:, col:col + 1]),
              reads=["wsum"], writes=["wsum16"])
            E(V, lambda v, col=col: v.scalar_tensor_tensor(wsumlo[:], wsum[:, col:col + 1], 1.0,
                                                           wsum16[:], ALU.mult, ALU.subtract),
              reads=["wsum", "wsum16"], writes=["wsumlo"])
            E(T, lambda t: t.matmul(psD[0:1, 0:1], wsum16[:], onesc[:], start=True, stop=False),
              reads=["wsum16", "onesc"], writes=[D_], tag=f"ms{tag}")
            E(T, lambda t: t.matmul(psD[0:1, 0:1], wsumlo[:], onesc[:], start=False, stop=True),
              reads=["wsumlo", "onesc"], writes=[D_], tag=f"ms2{tag}")
            E(V, lambda v, col=col, n_el=n_el: v.tensor_scalar(mrow[0:1, col:col + 1], psD[0:1, 0:1],
                                                               1.0 / n_el, EPS, ALU.mult, ALU.max),
              reads=[D_], writes=["mrow"], tag=f"mf{tag}")

        mean_finish(0, float(I * D), "g")
        mean_finish(1, float(I * D), "u")
        mean_finish(2, float(D * I), "d")
        bcast_row("mrow", mrow[0:1, 0:3], mrow16, "m16", mrowlo, "mlo", "mB", mB, 3, (psD, D_), "mb")
        E(V, lambda v: v.reciprocal(rmB[:, 0:3], mB[:, 0:3]), reads=["mB"], writes=["rmB"])

        em.fence()
        ABa.set_base(AB_P)
        iaT = ABa.alloc(DC * S_own, (DC, S_own))
        AB_P1B = ABa.cur
        qbf = ABa.alloc(D)
        ABa.set_base(AB_P1B)
        wt_bf = ABa.alloc(2 * D, (2, D))
        wtT = ABa.alloc(2 * 2 * DC * 128, (2, 2, DC, 128))
        hi_s = ABa.alloc(S_own)
        lo_s = ABa.alloc(S_own)
        arow16 = ABa.alloc(S_own, rows=1)
        arowlo = ABa.alloc(S_own, rows=1)
        AFa.set_base(0)
        aivB = AFa.alloc(S_own)
        arow = AFa.alloc(S_own, rows=1)
        xnq = AFa.alloc(D)
        sgn = AFa.alloc(D)
        qint = AIa.alloc(D // 2, off=0)

        d_iq = dbg("iq", [TT, 128, D], BF16)
        for i in range(TT):
            xk = ("xnq",)
            DMA("sync", i % 2, xnq, xn_scr.ap().rearrange("(t p) d -> t p d", p=128)[i],
                reads=[("xn_scr",)], writes=[xk], tag=f"ldxn{i}")
            E(V, lambda v: v.tensor_reduce(amaxv[:], xnq, AX.X, ALU.max, apply_absolute_value=True),
              reads=[xk], writes=["amaxv"], tag="amax")
            E(V, lambda v: v.tensor_scalar_max(amaxv[:], amaxv[:], EPS), reads=["amaxv"], writes=["amaxv"])
            E(V, lambda v, i=i: v.tensor_scalar_mul(ainv_all[:, i:i + 1], amaxv[:], 1.0 / 128.0),
              reads=["amaxv"], writes=["ainv_all"])
            E(V, lambda v: v.reciprocal(sclv[:], amaxv[:]), reads=["amaxv"], writes=["sclv"])
            E(V, lambda v: v.tensor_scalar_mul(sclv[:], sclv[:], 128.0), reads=["sclv"], writes=["sclv"])
            E(S, lambda s: s.activation(sgn, xnq, AF.Sign), reads=[xk], writes=["sgn"])
            E(V, lambda v: v.tensor_scalar_mul(sgn, sgn, KofsB[:, 0:1]),
              reads=["sgn", "KofsB"], writes=["sgn"])
            E(V, lambda v: v.scalar_tensor_tensor(xnq, xnq, sclv[:, 0:1], sgn, ALU.mult, ALU.add),
              reads=[xk, "sclv", "sgn"], writes=[xk], tag="qz")
            for hq in range(2):
                E(V, lambda v, hq=hq: v.tensor_copy(qint, xnq[:, hq * (D // 2):(hq + 1) * (D // 2)]),
                  reads=[xk], writes=["qint"])
                E(V, lambda v, hq=hq: v.tensor_scalar(qbf[:, hq * (D // 2):(hq + 1) * (D // 2)], qint,
                                                      127.0, None, ALU.min),
                  reads=["qint"], writes=["qbf"], tag="qmin")
            if d_iq is not None:
                DMA("sync", 25, d_iq[i], qbf, reads=["qbf"])
            transpose16("qbf", qbf, "iaT", iaT[:, :, i * 128:(i + 1) * 128],
                        DC, f"ia{i}", psT, T_)
        DMA("sync", 21, rowscr.ap()[0].rearrange("t p -> p t"), ainv_all[0:128, 0:TT],
            reads=["ainv_all"], writes=[("rowscr", 0)], tag="aivst", slow_ok=True)
        DMA("sync", 21, arow, rowscr.ap()[0:1, :, :].rearrange("a t p -> a (t p)"),
            reads=[("rowscr", 0)], writes=["arow"], tag="aivld")
        bcast_row("arow", arow, arow16, "a16", arowlo, "alo", "aivB", aivB, S_own, (psC, C_), "aiv")

        em.fence()
        AFa.set_base(2 * S_own)
        wf_slab2 = AFa.alloc(4 * D, (4, D))
        wsgn1 = AFa.alloc(2 * D, (2, D))
        wsgn2 = AFa.alloc(2 * D, (2, D))
        gate_s = AFa.alloc(S_own)
        up_s = AFa.alloc(S_own)
        sg_s = AFa.alloc(S_own)
        y1st = AFa.alloc(2 * S_own, (2, S_own))
        d_gate = dbg("gate", [1, 128, S_own])
        d_y1 = dbg("y1", [2, 128, S_own])
        def ldslab(oc):
            for mi, Wap in enumerate((Wg, Wu)):
                sl4 = 2 * mi + oc % 2
                DMA("sync", 4 + sl4, wf_slab2[:, sl4, :],
                    Wap.rearrange("(s p) i -> s p i", p=128)[oc], writes=[("wfb", sl4)],
                    tag=f"ldgu{oc}_{mi}")
        def tern_oc(oc):
            for mi, mcol in ((0, 0), (1, 1)):
                sl4 = 2 * mi + oc % 2
                ternarize(("wfb", sl4), wf_slab2[:, sl4, :], ("wt_bf", mi), wt_bf[:, mi, :], D, mcol,
                          wsgn1[:, mi, :], ("wsg1", mi), wsgn2[:, mi, :], ("wsg2", mi), f"t{oc}_{mi}")
        ldslab(0)
        if AC > 1:
            ldslab(1)
        tern_oc(0)

        def h1_te(oc):
            for ns in range(NS):
                pd = (psC, C_) if ns % 2 == 0 else (psD, D_)
                E(T, lambda t, ns=ns, pd=pd: t.matmul(pd[0][:, 0:ST], sh128[:],
                                                      hi_s[:, ns * ST:(ns + 1) * ST],
                                                      start=True, stop=False),
                  reads=["sh128", "hi_s"], writes=[pd[1]], tag="h1mm")
                E(T, lambda t, ns=ns, pd=pd: t.matmul(pd[0][:, 0:ST], sh128[:],
                                                      lo_s[:, ns * ST:(ns + 1) * ST],
                                                      start=False, stop=True),
                  reads=["sh128", "lo_s"], writes=[pd[1]], tag="h1mm2")

        def h1_ev(oc):
            for ns in range(NS):
                pd = (psC, C_) if ns % 2 == 0 else (psD, D_)
                E(V, lambda v, ns=ns, pd=pd, oc=oc: v.tensor_copy(y1st[:, oc % 2, ns * ST:(ns + 1) * ST],
                                                                  pd[0][:, 0:ST]),
                  reads=[pd[1]], writes=[("y1st", oc % 2)], tag="y1ev")
            DMA("sync", 12 + oc % 2, y1_scr.ap()[:, oc, :], y1st[:, oc % 2, :],
                reads=[("y1st", oc % 2)], writes=[("y1_scr",)], tag=f"sty1{oc}")
            if d_y1 is not None and oc < 2:
                DMA("sync", 25, d_y1[oc], y1st[:, oc % 2, :], reads=[("y1st", oc % 2)])

        def mm_chain(oc, mi):
            buf = oc % 2
            pdst, pkey = ((psA, A_) if mi == 0 else (psB, B_))
            for c in range(DC):
                for ns in range(NS):
                    E(T, lambda t, c=c, ns=ns, buf=buf, mi=mi, pdst=pdst: t.matmul(
                        pdst[:, ns * ST:(ns + 1) * ST], wtT[:, buf, mi, c, :],
                        iaT[:, c, ns * ST:(ns + 1) * ST], start=(c == 0), stop=(c == DC - 1)),
                      reads=[("wtT", buf, mi), "iaT"], writes=[pkey], tag=f"mm{mi}")

        for oc in range(AC):
            buf = oc % 2
            for mi in (0, 1):
                transpose16(("wt_bf", mi), wt_bf[:, mi, :], ("wtT", buf, mi), wtT[:, buf, mi, :, :],
                            DC, f"wt{oc}_{mi}", psT if mi == 0 else psU, T_ if mi == 0 else U_)
            mm_chain(oc, 0)
            if oc > 0:
                h1_te(oc - 1)
            mm_chain(oc, 1)
            if oc + 1 < AC:
                tern_oc(oc + 1)
            if oc > 0:
                h1_ev(oc - 1)
            if oc + 2 < AC:
                ldslab(oc + 2)
            E(V, lambda v: v.scalar_tensor_tensor(gate_s, psA[:, 0:S_own], mB[:, 0:1],
                                                  aivB, ALU.mult, ALU.mult),
              reads=[A_, "mB", "aivB"], writes=["gate_s"], tag="gev")
            E(S, lambda s: s.activation(sg_s, gate_s, AF.Silu),
              reads=["gate_s"], writes=["sg_s"])
            E(V, lambda v: v.scalar_tensor_tensor(up_s, psB[:, 0:S_own], mB[:, 1:2],
                                                  aivB, ALU.mult, ALU.mult),
              reads=[B_, "mB", "aivB"], writes=["up_s"], tag="uev")
            E(V, lambda v: v.tensor_mul(up_s, up_s, sg_s),
              reads=["up_s", "sg_s"], writes=["up_s"])
            if d_gate is not None and oc == 0:
                DMA("sync", 25, d_gate[0], gate_s, reads=["gate_s"])
            E(V, lambda v: v.tensor_copy(hi_s, up_s), reads=["up_s"], writes=["hi_s"])
            E(V, lambda v: v.scalar_tensor_tensor(lo_s, up_s, 1.0, hi_s, ALU.mult, ALU.subtract),
              reads=["up_s", "hi_s"], writes=["lo_s"])
        h1_te(AC - 1)
        h1_ev(AC - 1)

        em.fence()
        ABa.set_base(AB_P)
        y2b = ABa.alloc(AC * FWIN, (AC, FWIN))
        zrow16 = ABa.alloc(FWIN, rows=1)
        zrowlo = ABa.alloc(FWIN, rows=1)
        AFa.set_base(0)
        y1wD = AFa.alloc(2 * AC * FWIN, (2, AC, FWIN))
        zchD = AFa.alloc(2 * 4 * FWIN, (2, 4, FWIN))
        ymaxD = AFa.alloc(2 * FWIN, (2, FWIN))
        zsBD = AFa.alloc(2 * FWIN, (2, FWIN))
        zrowD = AFa.alloc(2 * FWIN, (2, FWIN), rows=1)
        y2icD = AIa.alloc(8 * FWIN, (2, 4, FWIN), off=0)

        d_scr2 = dbg("scr2", [128, AC, FWIN])
        if d_scr2 is not None:
            DMA("sync", 25, d_scr2, y1_scr.ap()[:, :, 2 * FWIN:3 * FWIN], reads=[("y1_scr",)])
        d_y2 = dbg("y2", [2, 128, FWIN], BF16)
        NBST = int(np.log2(AC))
        NSUBW = FWIN // 128
        def p1c_front(w):
            wp = w % 2
            y1w = y1wD[:, wp, :, :]
            yk1 = ("y1w", wp)
            ymax = ymaxD[:, wp, :]
            ymk = ("ymax", wp)
            for pc in range(4):
                a0, a1 = pc * (AC // 4), (pc + 1) * (AC // 4)
                DMA("sync", 12 + pc % 2, y1w[:, a0:a1, :], y1_scr.ap()[:, a0:a1, w * FWIN:(w + 1) * FWIN],
                    reads=[("y1_scr",)], writes=[yk1], tag=f"ldy1w{w}_{pc}")
            d_y1w = dbg("y1w2", [128, AC, FWIN]) if w == 2 else None
            if d_y1w is not None:
                DMA("sync", 25, d_y1w, y1w, reads=[yk1])
            for st in range(NBST):
                off = 2 ** st
                y4 = y1w.rearrange("p (blk two off) f -> p blk two off f", two=2, off=off)
                A4 = y4[:, :, 0, :, :]
                B4 = y4[:, :, 1, :, :]
                E(V, lambda v, A4=A4, B4=B4: v.tensor_add(A4, A4, B4), reads=[yk1],
                  writes=[yk1], tag=f"bfa{w}_{st}")
                E(V, lambda v, A4=A4, B4=B4: v.scalar_tensor_tensor(B4, B4, -2.0, A4, ALU.mult, ALU.add),
                  reads=[yk1], writes=[yk1], tag=f"bfs{w}_{st}")
            E(V, lambda v, y1w=y1w, ymax=ymax: v.tensor_reduce(ymax, y1w.rearrange("p a f -> p f a"),
                                                                AX.X, ALU.max, apply_absolute_value=True),
              reads=[yk1], writes=[ymk], tag="ymaxa")

        def p1c_tail(w):
            wp = w % 2
            y1w = y1wD[:, wp, :, :]
            yk1 = ("y1w", wp)
            ymax = ymaxD[:, wp, :]
            zsB = zsBD[:, wp, :]
            zrow = zrowD[0:1, wp, :]
            ymk, zbk, zrk = ("ymax", wp), ("zsB", wp), ("zrow", wp)
            for sw in range(NSUBW):
                tt = w * NSUBW + sw
                E(T, lambda t, sw=sw, ymax=ymax: t.transpose(psA[:, 0:128],
                                                             ymax[:, sw * 128:(sw + 1) * 128], sid32[:]),
                  reads=[ymk, "sid32"], writes=[A_], tag="ymtr")
                E(V, lambda v, wp=wp: v.tensor_reduce(tmax[:, wp:wp + 1], psA[:, 0:128], AX.X, ALU.max),
                  reads=[A_], writes=[("tmax", wp)], tag="ymaxb")
                E(V, lambda v, wp=wp: v.tensor_scalar(tmax[:, wp:wp + 1], tmax[:, wp:wp + 1], C_FW, EPS,
                                                      ALU.mult, ALU.max),
                  reads=[("tmax", wp)], writes=[("tmax", wp)])
                E(V, lambda v, tt=tt, wp=wp: v.tensor_scalar_mul(s2i_all[:, tt:tt + 1], tmax[:, wp:wp + 1],
                                                                 1.0 / 128.0),
                  reads=[("tmax", wp)], writes=["s2i_all"])
                E(V, lambda v, wp=wp: v.reciprocal(zs_tm[:, wp:wp + 1], tmax[:, wp:wp + 1]),
                  reads=[("tmax", wp)], writes=[("zs_tm", wp)])
                E(V, lambda v, wp=wp: v.tensor_scalar_mul(zs_tm[:, wp:wp + 1], zs_tm[:, wp:wp + 1],
                                                          128.0 * C_FW),
                  reads=[("zs_tm", wp)], writes=[("zs_tm", wp)])
                E(T, lambda t, wp=wp: t.transpose(psA[0:1, 128:256], zs_tm[:, wp:wp + 1], sid32[:]),
                  reads=[("zs_tm", wp), "sid32"], writes=[A_], tag="zstr")
                E(V, lambda v, sw=sw, zrow=zrow: v.tensor_copy(zrow[0:1, sw * 128:(sw + 1) * 128],
                                                               psA[0:1, 128:256]),
                  reads=[A_], writes=[zrk], tag="zsev")
            bcast_row(zrk, zrow, zrow16, "z16", zrowlo, "zlo", zbk, zsB, FWIN, (psB, B_), f"zs{w}")
            for a4 in range(0, AC, 4):
                gp = (a4 // 4) % 2
                zch = zchD[:, gp, :, :]
                zk = ("zch", gp)
                E(S, lambda s, a4=a4, y1w=y1w, zch=zch: s.activation(zch, y1w[:, a4:a4 + 4, :], AF.Sign),
                  reads=[yk1], writes=[zk], tag="sgn2")
                E(S, lambda s, zch=zch: s.activation(zch, zch, AF.Copy, scale=KofsB[:, 0:1]),
                  reads=[zk, "KofsB"], writes=[zk], tag="ko2")
                zsb4 = zsB.rearrange("p (a f) -> p a f", a=1).broadcast_to([128, 4, FWIN])
                E(V, lambda v, a4=a4, y1w=y1w, zsb4=zsb4: v.tensor_mul(y1w[:, a4:a4 + 4, :],
                                                                       y1w[:, a4:a4 + 4, :], zsb4),
                  reads=[yk1, zbk], writes=[yk1], tag="zmul")
                E(V, lambda v, a4=a4, y1w=y1w, zch=zch, gp=gp: v.scalar_tensor_tensor(
                    y2icD[:, gp, :, :], y1w[:, a4:a4 + 4, :], 127.0,
                    zch, ALU.min, ALU.add),
                  reads=[yk1, zk], writes=[("y2ic", gp)], tag="zqf")
                E(G, lambda g, a4=a4, gp=gp: g.tensor_copy(y2b[:, a4:a4 + 4, :], y2icD[:, gp, :, :]),
                  reads=[("y2ic", gp)], writes=["y2b"], tag="y2min")
            for pc in range(4):
                a0, a1 = pc * (AC // 4), (pc + 1) * (AC // 4)
                DMA("sync", 14 + pc % 2, y2_scr.ap()[:, a0:a1, w * FWIN:(w + 1) * FWIN], y2b[:, a0:a1, :],
                    reads=["y2b"], writes=[("y2_scr",)], tag=f"sty2{w}_{pc}")
            if d_y2 is not None and w == 0:
                DMA("sync", 25, d_y2[0], y2b[:, 0, :], reads=["y2b"])
                DMA("sync", 25, d_y2[1], y2b[:, 1, :], reads=["y2b"])

        p1c_front(0)
        if NW > 1:
            p1c_front(1)
        for w in range(NW):
            p1c_tail(w)
            if w + 2 < NW:
                p1c_front(w + 2)

        em.fence()
        ABa.set_base(AB_P)
        wd_bfq = ABa.alloc(I, (4, I // 4))
        y2rhs = ABa.alloc(3 * 2 * S_own, (3, 2, S_own))
        wdT = ABa.alloc(2 * AC * 128, (2, AC, 128))
        ar16d = ABa.alloc(512, rows=1)
        arlod = ABa.alloc(512, rows=1)
        AFa.set_base(0)
        wd_slab2q = AFa.alloc(I, (4, I // 4))
        wdsg1 = AFa.alloc(I // 4)
        wdsg2 = AFa.alloc(I // 4)
        down_s = AFa.alloc(2 * S_own, (2, S_own))
        gsil = AFa.alloc(S_own)
        gstg = AFa.alloc(2 * 128, (2, 128))
        s2ivB = AFa.alloc(S_own)
        arow2 = AFa.alloc(S_own, rows=1)

        DMA("sync", 21, rowscr.ap()[1].rearrange("t p -> p t"), s2i_all[0:128, 0:TT],
            reads=["s2i_all"], writes=[("rowscr", 1)], tag="s2st", slow_ok=True)
        DMA("sync", 21, arow2, rowscr.ap()[1:2, :, :].rearrange("a t p -> a (t p)"),
            reads=[("rowscr", 1)], writes=["arow2"], tag="s2ld")
        CHb = min(512, S_own)
        for j4 in range(S_own // CHb):
            bcast_row("arow2", arow2[0:1, j4 * CHb:(j4 + 1) * CHb], ar16d, "a16d", arlod, "alod",
                      "s2ivB", s2ivB[:, j4 * CHb:(j4 + 1) * CHb], CHb, (psC, C_), f"s2b{j4}")
        d_s2b = dbg("s2b", [128, S_own])
        if d_s2b is not None:
            DMA("sync", 25, d_s2b, s2ivB, reads=["s2ivB"])
        d_s2a = dbg("s2a", [128, max(TT, 2)])
        if d_s2a is not None:
            DMA("sync", 25, d_s2a, s2i_all[:], reads=["s2i_all"])
        d_ar2 = dbg("ar2", [1, S_own])
        if d_ar2 is not None:
            DMA("sync", 25, d_ar2, arow2, reads=["arow2"])

        d_dn = dbg("dn", [128, S_own])
        odgrps = [list(range(g0, min(g0 + 2, D // 128))) for g0 in range(0, D // 128, 2)]
        QW = I // 4
        QC = QW // 128

        def ldwd(od, q):
            buf = (od % 2) * 2 + q % 2
            DMA("sync", 8 + (2 * (od % 2) + q) % 4, wd_slab2q[:, buf, :],
                Wd.rearrange("(s p) i -> s p i", p=128)[od][:, q * QW:(q + 1) * QW],
                writes=[("wdb", buf)], tag=f"lwd{od}_{q}")

        def tern_unit(od, q):
            buf = (od % 2) * 2 + q % 2
            ternarize(("wdb", buf), wd_slab2q[:, buf, :], ("wdq", buf), wd_bfq[:, buf, :],
                      QW, 2, wdsg1, "wdsg1", wdsg2, "wdsg2", f"d{od}_{q}")

        def transp_unit(od, q):
            j = od % 2
            buf = j * 2 + q % 2
            transpose16(("wdq", buf), wd_bfq[:, buf, :], ("wdT", j, q),
                        wdT[:, j, q * QC:(q + 1) * QC, :],
                        QC, f"wdt{od}_{q}", psT if j == 0 else psU, T_ if j == 0 else U_)

        def prep_group(grp):
            for q in (0, 1):
                for od in grp:
                    ldwd(od, q)
            for q in range(4):
                for od in grp:
                    tern_unit(od, q)
                if q + 2 < 4:
                    for od in grp:
                        ldwd(od, q + 2)
                for od in grp:
                    transp_unit(od, q)

        y2items = [(gi, a0) for gi in range(len(odgrps)) for a0 in range(0, AC, 2)]
        y2ptr = [0]

        def ldy2_next():
            if y2ptr[0] >= len(y2items):
                return
            gi_, a0_ = y2items[y2ptr[0]]
            bi = y2ptr[0] % 3
            y2ptr[0] += 1
            DMA("sync", 14 + bi, y2rhs[:, bi, :, :], y2_scr.ap()[:, a0_:a0_ + 2, :],
                reads=[("y2_scr",)], writes=[("y2r", bi)], tag=f"ly2_{gi_}_{a0_}")

        prep_group(odgrps[0])
        for gi, grp in enumerate(odgrps):
            nxt = odgrps[gi + 1] if gi + 1 < len(odgrps) else None
            acts_pre, acts_post = {}, {}
            if nxt is not None:
                acts_pre.setdefault(0, []).extend(("ld", od, 0) for od in nxt)
                acts_pre.setdefault(0, []).extend(("ld", od, 1) for od in nxt)
                for q in range(4):
                    acts_pre.setdefault(16 * q + 8, []).extend(("tern", od, q) for od in nxt)
                    if q + 2 < 4:
                        acts_pre.setdefault(16 * q + 10, []).extend(("ld", od, q + 2) for od in nxt)
                    acts_post.setdefault(16 * q + 14, []).extend(("tr", od, q) for od in nxt)

            def do(acts):
                for kind, od, q in acts:
                    if kind == "ld":
                        ldwd(od, q)
                    elif kind == "tern":
                        tern_unit(od, q)
                    else:
                        transp_unit(od, q)

            for a0 in range(0, AC, 2):
                gc = gi * (AC // 2) + a0 // 2
                while y2ptr[0] <= gc + 2 and y2ptr[0] < len(y2items):
                    ldy2_next()
                do(acts_pre.get(a0, ()))
                rk = ("y2r", gc % 3)
                for a in range(a0, a0 + 2):
                    for j, od in enumerate(grp):
                        pd = (psA, A_) if j == 0 else (psB, B_)
                        for ns in range(NS):
                            E(T, lambda t, a=a, a0=a0, gc=gc, j=j, ns=ns, pd=pd: t.matmul(
                                pd[0][:, ns * ST:(ns + 1) * ST], wdT[:, j, a, :],
                                y2rhs[:, gc % 3, a - a0, ns * ST:(ns + 1) * ST],
                                start=(a == 0), stop=(a == AC - 1)),
                              reads=[("wdT", j, a // QC), rk], writes=[pd[1]], tag="dmm")
                do(acts_post.get(a0, ()))
            for ns in range(NS):
                for j, od in enumerate(grp):
                    jp = j % 2
                    pd = (psA, A_) if j == 0 else (psB, B_)
                    E(V, lambda v, ns=ns, pd=pd, jp=jp: v.scalar_tensor_tensor(
                        down_s[:, jp, ns * ST:(ns + 1) * ST], pd[0][:, ns * ST:(ns + 1) * ST], mB[:, 2:3],
                        s2ivB[:, ns * ST:(ns + 1) * ST], ALU.mult, ALU.mult),
                      reads=[pd[1], "mB", "s2ivB"], writes=[("down_s", jp)], tag="dev")
                    E(S, lambda s, ns=ns, jp=jp: s.activation(gsil[:, ns * ST:(ns + 1) * ST],
                                                              down_s[:, jp, ns * ST:(ns + 1) * ST], AF.Silu),
                      reads=[("down_s", jp)], writes=["gsil"], tag="dsil")
                    if d_dn is not None and od == 0:
                        DMA("sync", 25, d_dn[:, ns * ST:(ns + 1) * ST],
                            down_s[:, jp, ns * ST:(ns + 1) * ST], reads=[("down_s", jp)])
                    for k in range(ST // 128):
                        tt = (ns * ST) // 128 + k
                        E(T, lambda t, k=k, ns=ns: t.transpose(
                            psD[:, (k % 4) * 128:(k % 4 + 1) * 128],
                            gsil[:, ns * ST + k * 128:ns * ST + (k + 1) * 128], sid32[:]),
                          reads=["gsil", "sid32"], writes=[D_], tag="gtr")
                        E(S, lambda s, k=k, tt=tt: s.copy(gstg[:, tt % 2, :],
                                                          psD[:, (k % 4) * 128:(k % 4 + 1) * 128]),
                          reads=[D_], writes=[("gstg", tt % 2)], tag="gtev")
                        DMA("sync", 18 + tt % 2,
                            g_scr.ap().rearrange("(t p) d -> t p d", p=128)[tt][:, od * 128:(od + 1) * 128],
                            gstg[:, tt % 2, :], reads=[("gstg", tt % 2)], writes=[("g_scr",)],
                            tag=f"stg{od}_{tt}")

        em.fence()
        ABa.set_base(AB_P)
        attnU = ABa.alloc(TA * S_own, (TA, S_own))
        oh = ABa.alloc(TT * H * 132, (TT, H, 132))

        d_oh = dbg("oh", [128, TT, H, 132], BF16)
        if d_oh is not None:
            E(V, lambda v: v.memset(oh, 0.0), writes=["oh"])
        for h in range(H):
            for kc in range(TA):
                pd = (psA, A_) if kc % 2 == 0 else (psB, B_)
                for ns in range(NS):
                    E(T, lambda t, h=h, kc=kc, ns=ns, pd=pd: t.matmul(
                        pd[0][:, ns * ST:(ns + 1) * ST], khT[:, kc * 128:(kc + 1) * 128],
                        qhT[:, h, ns * ST:(ns + 1) * ST], start=True, stop=True),
                      reads=["khT", "qhT"], writes=[pd[1]], tag="smm")
                E(S, lambda s, kc=kc, pd=pd: s.activation(attnU[:, kc, :], pd[0][:, 0:S_own], AF.Exp),
                  reads=[pd[1]], writes=[("aU", kc)], tag="expev")
            for q0 in range(0, TT, 2):
                qs = list(range(q0, min(q0 + 2, TT)))
                for kc in range(TA):
                    for j, q in enumerate(qs):
                        pd = (psC, C_) if j == 0 else (psD, D_)
                        E(T, lambda t, kc=kc, q=q, pd=pd: t.matmul(
                            pd[0][:, 0:129], attnU[:, kc, q * 128:(q + 1) * 128],
                            vT1[:, kc, 0:129], start=(kc == 0), stop=(kc == TA - 1)),
                          reads=[("aU", kc), ("vT1", kc), "vT1ones"], writes=[pd[1]], tag="avmm")
                for j, q in enumerate(qs):
                    pd = (psC, C_) if j == 0 else (psD, D_)
                    E(V, lambda v, pd=pd: v.reciprocal(recips[:], pd[0][:, 128:129]),
                      reads=[pd[1]], writes=["recips"], tag="avrc")
                    E(S, lambda s, pd=pd, q=q, h=h: s.activation(oh[:, q, h, 0:128], pd[0][:, 0:128],
                                                                 AF.Copy, scale=recips[:, 0:1]),
                      reads=[pd[1], "recips"], writes=["oh"], tag="ohev")
        if d_oh is not None:
            DMA("sync", 25, d_oh, oh, reads=["oh"])

        em.fence()
        wq_slab2 = ABa.alloc(2 * DC * 128, (2, DC, 128), off=0)
        ABa.set_base(AB_P)
        ohT = ABa.alloc(DC * S_own, (DC, S_own))
        AFa.set_base(0)
        wo_evD = AFa.alloc(2 * S_own, (2, S_own))
        aostg = AFa.alloc(8 * 128, (8, 128))

        for tt in range(TT):
            for h in range(H):
                jj = (tt * H + h) % 8
                E(T, lambda t, tt=tt, h=h, jj=jj: t.transpose(psT[:, jj * 128:(jj + 1) * 128],
                                                              oh[:, tt, h, 0:128], sid[:]),
                  reads=["oh", "sid"], writes=[T_], tag="ohtr")
                E(V, lambda v, tt=tt, h=h, jj=jj: v.tensor_copy(ohT[:, h, tt * 128:(tt + 1) * 128],
                                                                psT[:, jj * 128:(jj + 1) * 128]),
                  reads=[T_], writes=["ohT"], tag="ohtev")
        wq_stg = AFa.alloc(2 * DC * 128, (2, DC, 128))

        def ldwo(oc):
            DMA("sync", 12 + oc % 2, wq_stg[:, oc % 2, :, :],
                Wout[:, oc * 128:(oc + 1) * 128].rearrange("(c p) m -> p c m", p=128),
                writes=[("wqs", oc % 2)], tag=f"ldwo{oc}")
            E(V, lambda v, oc=oc: v.tensor_copy(wq_slab2[:, oc % 2, :, :], wq_stg[:, oc % 2, :, :]),
              reads=[("wqs", oc % 2)], writes=[("wq", oc % 2)], tag=f"cwo{oc}")
        ldwo(0)
        if DC > 1:
            ldwo(1)
        def ao_out(oc):
            for tt in range(TT):
                E(T, lambda t, tt=tt, oc=oc: t.transpose(psB[:, (tt % 8) * 128:(tt % 8 + 1) * 128],
                                                         wo_evD[:, oc % 2, tt * 128:(tt + 1) * 128],
                                                         sid32[:]),
                  reads=[("wo_ev", oc % 2), "sid32"], writes=[B_], tag="aotr")
                E(S, lambda s, tt=tt: s.copy(aostg[:, tt, :],
                                             psB[:, (tt % 8) * 128:(tt % 8 + 1) * 128]),
                  reads=[B_], writes=[("aostg", tt)], tag="aoev")
                DMA("sync", 20 + tt % 4,
                    ao_scr.ap().rearrange("(t p) d -> t p d", p=128)[tt][:, oc * 128:(oc + 1) * 128],
                    aostg[:, tt, :], reads=[("aostg", tt)], writes=[("ao_scr",)],
                    tag=f"stao{oc}_{tt}")

        for oc in range(DC):
            wk = ("wq", oc % 2)
            for c in range(DC):
                for ns in range(NS):
                    E(T, lambda t, c=c, ns=ns, oc=oc: t.matmul(
                        psA[:, ns * ST:(ns + 1) * ST], wq_slab2[:, oc % 2, c, :],
                        ohT[:, c, ns * ST:(ns + 1) * ST], start=(c == 0), stop=(c == DC - 1)),
                      reads=[wk, "ohT"], writes=[A_], tag="womm")
            E(V, lambda v, oc=oc: v.tensor_copy(wo_evD[:, oc % 2, :], psA[:, 0:S_own]),
              reads=[A_], writes=[("wo_ev", oc % 2)])
            if oc > 0:
                ao_out(oc - 1)
            if oc + 2 < DC:
                ldwo(oc + 2)
        ao_out(DC - 1)

        em.fence()
        AFa.set_base(0)
        aot = AFa.alloc(2 * D, (2, D))
        goB = AFa.alloc(D)
        stileD = AFa.alloc(2 * D, (2, D))
        xrb = AFa.alloc(2 * D, (2, D))
        grb = AFa.alloc(2 * D, (2, D))
        outt = AFa.alloc(2 * D, (2, D))
        DMA("sync", 26, goB, g_outB, writes=["goB"])
        d_ao = dbg("ao", [TT, 128, D])

        def p3_front(tt):
            DMA("sync", 0 + tt % 2, aot[:, tt % 2, :],
                ao_scr.ap().rearrange("(t p) d -> t p d", p=128)[tt],
                reads=[("ao_scr",)], writes=[("aot", tt % 2)], tag=f"lao{tt}")
            DMA("sync", 4 + tt % 2, xrb[:, tt % 2, :],
                x_own.rearrange("(t p) d -> t p d", p=128)[tt], writes=[("xrb", tt % 2)], tag=f"lx2_{tt}")
            DMA("sync", 6 + tt % 2, grb[:, tt % 2, :],
                g_scr.ap().rearrange("(t p) d -> t p d", p=128)[tt], reads=[("g_scr",)],
                writes=[("grb", tt % 2)], tag=f"lg2_{tt}")
            if d_ao is not None:
                DMA("sync", 25, d_ao[tt], aot[:, tt % 2, :], reads=[("aot", tt % 2)])
            ln_stats(("aot", tt % 2), aot[:, tt % 2, :], f"l3_{tt}", tt % 2)

        def p3_tail(tt):
            stile = stileD[:, tt % 2, :]
            sk = ("stile", tt % 2)
            E(V, lambda v, tt=tt, stile=stile: v.tensor_scalar(stile, aot[:, tt % 2, :],
                                                               rstd[:, tt % 2:tt % 2 + 1],
                                                               mbv[:, tt % 2:tt % 2 + 1],
                                                               ALU.mult, ALU.add),
              reads=[("aot", tt % 2), ("rstd", tt % 2), ("mbv", tt % 2)], writes=[sk], tag="ln3ap")
            E(V, lambda v, stile=stile: v.tensor_mul(stile, stile, goB),
              reads=[sk, "goB"], writes=[sk])
            E(V, lambda v, tt=tt, stile=stile: v.tensor_mul(outt[:, tt % 2, :], stile, grb[:, tt % 2, :]),
              reads=[sk, ("grb", tt % 2)], writes=[("outt", tt % 2)], tag="sg")
            E(V, lambda v, tt=tt: v.tensor_add(outt[:, tt % 2, :], outt[:, tt % 2, :], xrb[:, tt % 2, :]),
              reads=[("outt", tt % 2), ("xrb", tt % 2)], writes=[("outt", tt % 2)], tag="resid")
            DMA("sync", 22 + tt % 2, OUT.rearrange("(t p) d -> t p d", p=128)[tt], outt[:, tt % 2, :],
                reads=[("outt", tt % 2)], tag=f"stout{tt}")

        p3_front(0)
        if TT > 1:
            p3_front(1)
        for tt in range(TT):
            p3_tail(tt)
            if tt + 2 < TT:
                p3_front(tt + 2)

        em.barrier("sync")
        block = es.enter_context(nc.Block())
        stats = em.finalize(block)
    return nc, stats, sorted(dbgs.keys())




def _hadamard(n):
    h = np.array([[1.0]], dtype=np.float32)
    while h.shape[0] < n:
        h = np.block([[h, h], [h, -h]])
    return h.astype(np.float32)


def kernel(L0_attn_x, L0_attn_g_norm, L0_attn_Wq, L0_attn_Wkv, L0_attn_Wout,
           L0_attn_g_out, L0_ffn_w_gate, L0_ffn_w_up, L0_ffn_w_down):
    X = np.ascontiguousarray(np.asarray(L0_attn_x, dtype=np.float32))
    B, N, Dm = X.shape
    half = N // 2
    cfg = Cfg(D=Dm, S_own=half, S_all=N, I=L0_ffn_w_gate.shape[0], FWIN=128)
    nc, stats, _ = build_program(cfg)

    consts = {
        "g_normB": np.ascontiguousarray(np.broadcast_to(np.asarray(L0_attn_g_norm, np.float32), (128, Dm))),
        "g_outB": np.ascontiguousarray(np.broadcast_to(np.asarray(L0_attn_g_out, np.float32), (128, Dm))),
        "Wq": np.ascontiguousarray(np.asarray(L0_attn_Wq, np.float32)),
        "Wkv": np.ascontiguousarray(np.asarray(L0_attn_Wkv, np.float32)),
        "Wout": np.ascontiguousarray(np.asarray(L0_attn_Wout, np.float32)),
        "Wg": np.ascontiguousarray(np.asarray(L0_ffn_w_gate, np.float32)),
        "Wu": np.ascontiguousarray(np.asarray(L0_ffn_w_up, np.float32)),
        "Wd": np.ascontiguousarray(np.asarray(L0_ffn_w_down, np.float32)),
        "IDN": np.eye(128, dtype=BF),
        "IDN32": np.eye(128, dtype=np.float32),
        "H128": _hadamard(128).astype(BF),
    }
    in_maps = []
    for c in range(8):
        b, hf = c // 2, c % 2
        m = dict(consts)
        m["x_own"] = np.ascontiguousarray(X[b, hf * half:(hf + 1) * half])
        m["x_oth"] = np.ascontiguousarray(X[b, (1 - hf) * half:(2 - hf) * half])
        in_maps.append(m)
    res = run_bass_kernel_spmd(nc, in_maps, core_ids=list(range(8)))
    out = np.empty((B, N, Dm), dtype=np.float32)
    for c in range(8):
        b, hf = c // 2, c % 2
        out[b, hf * half:(hf + 1) * half] = res.results[c]["OUT"]
    return out
